# version 05
# baseline (speedup 1.0000x reference)
import numpy as np
import ml_dtypes
from contextlib import ExitStack

import concourse.bass as bass
import concourse.mybir as mybir
from concourse.bass_utils import run_bass_kernel_spmd

F32 = mybir.dt.float32
BF16 = mybir.dt.bfloat16
AF = mybir.ActivationFunctionType
ALU = mybir.AluOpType
AX = mybir.AxisListType

NCORE = 8
D = 2048
ET = D // 128
NEXP = 64
TOPK = 8
EPS = 1e-5


class Buf:
    __slots__ = ("w", "rs")

    def __init__(self):
        self.w = None
        self.rs = {}


class Dep:

    ENGS = ("t", "v", "s", "g", "q")

    def __init__(self, nc, ctx):
        keys = ["t", "v", "s", "cc", "gb", "qt",
                "qx0", "qx1", "gw0", "gw1", "qk0", "qk1", "qv0", "qv1", "qv2",
                "yw0", "yw1", "sp0", "sp1", "kp0", "kp1"]
        self.sem = {k: ctx.enter_context(nc.semaphore("sem_" + k)) for k in keys}
        self.cnt = {k: 0 for k in keys}
        self.seen = {e: {k: 0 for k in keys} for e in self.ENGS}

    def wait(self, eng, ename, tok):
        if tok is None:
            return
        key, v = tok
        if self.seen[ename][key] < v:
            eng.wait_ge(self.sem[key], v)
            self.seen[ename][key] = v

    def rd(self, eng, ename, *bufs):
        for b in bufs:
            self.wait(eng, ename, b.w)

    def wr(self, eng, ename, *bufs):
        for b in bufs:
            self.wait(eng, ename, b.w)
            for t in b.rs.values():
                self.wait(eng, ename, t)

    def inc(self, instr, key, n=1):
        instr.then_inc(self.sem[key], n)
        self.cnt[key] += n
        return (key, self.cnt[key])

    @staticmethod
    def did_w(tok, *bufs):
        for b in bufs:
            b.w = tok
            b.rs = {}

    @staticmethod
    def did_r(tok, *bufs):
        for b in bufs:
            b.rs[tok[0]] = tok

    def guard_merge(self, *bufs):
        if not hasattr(self, "guard"):
            self.guard = {}
        for b in bufs:
            toks = list(b.rs.values()) + ([b.w] if b.w else [])
            for k, v in toks:
                if k not in self.guard or self.guard[k][1] < v:
                    self.guard[k] = (k, v)

    def newbuf(self):
        b = Buf()
        if hasattr(self, "guard"):
            b.rs = dict(self.guard)
        return b


def build(S_CORE=2048, G_ST=4, T_SUP=4):
    S_TOT = NCORE * S_CORE
    NST = S_CORE // 128
    NG = NST // G_ST
    SW = G_ST * 128
    assert SW <= 512
    TP = T_SUP * 128
    NSUP = S_TOT // TP
    assert S_CORE % TP == 0
    SUP_PER_CORE = S_CORE // TP
    REG = ET * SW
    assert REG == G_ST * D
    NCH = D // 512

    nc = bass.Bass("TRN2", target_bir_lowering=False)

    x_h = nc.dram_tensor("x", [S_CORE, D], F32, kind="ExternalInput")
    wq_h = nc.dram_tensor("wq", [ET, 128, ET, 128], F32, kind="ExternalInput")
    wk_h = nc.dram_tensor("wk", [ET, 128, ET, 128], F32, kind="ExternalInput")
    wv_h = nc.dram_tensor("wv", [ET, 128, D], F32, kind="ExternalInput")
    wo_h = nc.dram_tensor("wo", [ET, 128, D], F32, kind="ExternalInput")
    qb_h = nc.dram_tensor("qb", [128, ET], F32, kind="ExternalInput")
    kb_h = nc.dram_tensor("kb", [128, ET], F32, kind="ExternalInput")
    vb_h = nc.dram_tensor("vb", [1, D], F32, kind="ExternalInput")
    rw_h = nc.dram_tensor("rw", [ET, 128, NEXP], F32, kind="ExternalInput")
    rb_h = nc.dram_tensor("rb", [1, NEXP], F32, kind="ExternalInput")
    eb_h = nc.dram_tensor("eb", [NEXP, D], F32, kind="ExternalInput")
    cosq_h = nc.dram_tensor("cosq", [128, S_CORE], F32, kind="ExternalInput")
    sinq_h = nc.dram_tensor("sinq", [128, S_CORE], F32, kind="ExternalInput")
    cosk_h = nc.dram_tensor("cosk", [128, S_CORE], F32, kind="ExternalInput")
    sink_h = nc.dram_tensor("sink", [128, S_CORE], F32, kind="ExternalInput")
    idb_h = nc.dram_tensor("idb", [128, 128], BF16, kind="ExternalInput")
    rotm_h = nc.dram_tensor("rotm", [128, 128], BF16, kind="ExternalInput")
    onesb_h = nc.dram_tensor("onesb", [128, 128], BF16, kind="ExternalInput")
    onesf_h = nc.dram_tensor("onesf", [1, 1], F32, kind="ExternalInput")
    y_h = nc.dram_tensor("y", [S_CORE, D], F32, kind="ExternalOutput")

    KVN = D * S_CORE
    kv_in = nc.dram_tensor("kv_in", [2 * KVN], BF16, kind="Internal")
    k_out = nc.dram_tensor("k_out", [NCORE * KVN], BF16, kind="Internal",
                           addr_space="Shared")
    v_out = nc.dram_tensor("v_out", [NCORE * KVN], BF16, kind="Internal",
                           addr_space="Shared")

    with ExitStack() as ctx:
        en = ctx.enter_context
        dep = Dep(nc, ctx)
        block = en(nc.Block())

        def T(f):
            block.tensor(f)

        def V(f):
            block.vector(f)

        def S(f):
            block.scalar(f)

        def Gp(f):
            block.gpsimd(f)

        def Q(f):
            block.sync(f)

        idb = en(nc.sbuf_tensor("idb_s", [128, 128], BF16))
        rotm = en(nc.sbuf_tensor("rotm_s", [128, 128], BF16))
        onesb = en(nc.sbuf_tensor("onesb_s", [128, 128], BF16))
        onesf = en(nc.sbuf_tensor("onesf_s", [1, 1], F32))
        qb_s = en(nc.sbuf_tensor("qb_s", [128, ET], F32))
        kb_s = en(nc.sbuf_tensor("kb_s", [128, ET], F32))
        xt = [en(nc.sbuf_tensor(f"xt{i}", [128, D], F32)) for i in range(2)]
        b_wo_c0 = Buf()
        ln = en(nc.sbuf_tensor("ln", [128, 32], F32))

        b_xt = [Buf(), Buf()]
        b_ln = Buf()
        b_qTg = [Buf() for _ in range(NG)]
        b_kvin = Buf()
        b_consts = Buf()
        b_psA = Buf()
        b_psAh = [Buf(), Buf()]
        b_psB = [Buf(), Buf()]
        b_psRS = Buf()
        b_psSp = Buf()

        psA = en(nc.psum_tensor("psA", [128, 2048], F32))
        psB = en(nc.psum_tensor("psB", [128, 2048], F32))

        def consts(g):
            pairs = [(idb, idb_h), (rotm, rotm_h), (onesb, onesb_h),
                     (onesf, onesf_h), (qb_s, qb_h), (kb_s, kb_h)]
            t = None
            for dst, src in pairs:
                g.wait_ge(dep.sem["gb"], dep.cnt["gb"])
                t = dep.inc(g.dma_start(out=dst[:, :], in_=src[:, :]),
                            "gb", 16)
            dep.did_w(t, b_consts)

        Gp(consts)

        def preload_tbl(sc):
            from concourse.hw_specs import get_activation_tables
            for idx, nm in enumerate(get_activation_tables(nc.m.arch)):
                if nm == "natural_log_exp_and_others":
                    sc.add_instruction(mybir.InstLoadActFuncSet(
                        name=nc.get_next_instruction_name(),
                        act_func_set_id=idx, ins=[], outs=[]))
                    return
            raise ValueError("no natural_log_exp_and_others table set")

        S(preload_tbl)

        c12 = ctx.enter_context(ExitStack())
        xnT = c12.enter_context(nc.sbuf_tensor("xnT", [128, ET, S_CORE],
                                               BF16))
        b_xnT = Buf()
        with ExitStack() as c1:
            wv_s = c1.enter_context(nc.sbuf_tensor("wv_s", [128, ET, D], BF16))
            vb_s = c1.enter_context(nc.sbuf_tensor("vb_s", [1, D], BF16))
            xnt = [c1.enter_context(nc.sbuf_tensor(f"xnt{i}", [128, D], BF16))
                   for i in range(2)]
            scr = c1.enter_context(nc.sbuf_tensor("scr", [128, D], F32))
            b_xnt = [Buf(), Buf()]
            b_scr = Buf()
            vst = [c1.enter_context(nc.sbuf_tensor(f"vst{i}", [128, D], BF16))
                   for i in range(2)]
            b_wv = [Buf() for _ in range(4)]
            b_vst = [Buf(), Buf()]

            def wv_load(g):
                g.wait_ge(dep.sem["gb"], dep.cnt["gb"])
                dep.inc(g.dma_start(out=vb_s[:, :], in_=vb_h[:, :]), "gb", 16)
                for c in range(4):
                    g.wait_ge(dep.sem["gb"], dep.cnt["gb"])
                    t = dep.inc(g.dma_start(
                        out=wv_s[:, c * 4:(c + 1) * 4, :],
                        in_=wv_h[c * 4:(c + 1) * 4, :, :]
                        .rearrange("a p n -> p a n")), "gb", 16)
                    dep.did_w(t, b_wv[c])

            Gp(wv_load)

            def xload(q, st, sl):
                dep.wr(q, "q", b_xt[sl])
                t = dep.inc(q.dma_start(out=xt[sl][:, :],
                                        in_=x_h[st * 128:(st + 1) * 128, :]),
                            "qx%d" % sl, 16)
                dep.did_w(t, b_xt[sl])

            Q(lambda q: xload(q, 0, 0))

            for st in range(NST):
                sl = st % 2
                if st + 1 < NST:
                    Q(lambda q, st=st, sl=sl: xload(q, st + 1, 1 - sl))

                def ln1(v, sl=sl):
                    dep.rd(v, "v", b_xt[sl])
                    dep.wr(v, "v", b_ln, b_scr)
                    v.reduce_sum(ln[:, 0:1], xt[sl][:, :], axis=AX.X)
                    t = dep.inc(v.scalar_tensor_tensor(
                        scr[:, :], xt[sl][:, :], 1.0, xt[sl][:, :],
                        op0=ALU.mult, op1=ALU.mult, accum_out=ln[:, 1:2]), "v")
                    dep.did_w(t, b_scr)
                    dep.did_r(t, b_xt[sl])
                    v.wait_ge(dep.sem["v"], dep.cnt["v"])
                    v.tensor_scalar(ln[:, 2:3], ln[:, 0:1], 1.0 / D, None,
                                    op0=ALU.mult)
                    t = dep.inc(v.tensor_scalar(ln[:, 3:4], ln[:, 1:2],
                                                -1.0 / D, -EPS,
                                                op0=ALU.mult, op1=ALU.add), "v")
                    v.wait_ge(dep.sem["v"], dep.cnt["v"])
                    t = dep.inc(v.scalar_tensor_tensor(
                        ln[:, 4:5], ln[:, 2:3], ln[:, 2:3], ln[:, 3:4],
                        op0=ALU.mult, op1=ALU.add), "v")
                    dep.did_w(t, b_ln)

                V(ln1)

                def ln2(s, sl=sl):
                    dep.rd(s, "s", b_ln)
                    t = dep.inc(s.activation(ln[:, 5:6], ln[:, 4:5],
                                             AF.Ln, scale=-1.0), "s")
                    s.wait_ge(dep.sem["s"], t[1])
                    t = dep.inc(s.activation(ln[:, 6:7], ln[:, 5:6], AF.Exp,
                                             scale=-0.5), "s")
                    dep.did_w(t, b_ln)

                S(ln2)

                def ln3(v, sl=sl):
                    dep.rd(v, "v", b_ln)
                    t = dep.inc(v.scalar_tensor_tensor(
                        ln[:, 7:8], ln[:, 2:3], -1.0, ln[:, 6:7],
                        op0=ALU.mult, op1=ALU.mult), "v")
                    dep.did_w(t, b_ln)

                V(ln3)

                def ln4(s, sl=sl):
                    dep.rd(s, "s", b_ln, b_xt[sl])
                    dep.wr(s, "s", b_xnt[sl])
                    t = dep.inc(s.activation(xnt[sl][:, :], xt[sl][:, :],
                                             AF.Identity, bias=ln[:, 7:8],
                                             scale=ln[:, 6:7]), "s")
                    dep.did_w(t, b_xnt[sl])
                    dep.did_r(t, b_xt[sl], b_ln)

                S(ln4)

                for b4 in range(4):
                    slot = b4 % 2

                    def tr(t_, st=st, sl=sl, b4=b4, slot=slot):
                        dep.rd(t_, "t", b_xnt[sl], b_consts)
                        dep.wr(t_, "t", b_psB[slot])
                        for b in range(4):
                            jt = b4 * 4 + b
                            li = t_.matmul(psB[:, slot * 512 + b * 128:
                                          slot * 512 + (b + 1) * 128],
                                      xnt[sl][:, jt * 128:(jt + 1) * 128],
                                      idb[:, :], start=True, stop=True)
                        tok = dep.inc(li, "t")
                        dep.did_w(tok, b_psB[slot])
                        dep.did_r(tok, b_xnt[sl])

                    T(tr)

                    def ev(s, st=st, b4=b4, slot=slot):
                        dep.rd(s, "s", b_psB[slot])
                        dep.wr(s, "s", b_xnT)
                        tok = dep.inc(s.copy(
                            xnT[:, b4 * 4:(b4 + 1) * 4,
                                st * 128:(st + 1) * 128],
                            psB[:, slot * 512:(slot + 1) * 512]
                            .rearrange("p (a n) -> p a n", n=128)), "s")
                        b_xnT.w = tok
                        dep.did_r(tok, b_psB[slot])

                    S(ev)

                def vproj(t_, st=st):
                    dep.rd(t_, "t", b_xnT, b_consts)
                    dep.wr(t_, "t", b_psA)
                    for jt in range(ET):
                        dep.rd(t_, "t", b_wv[jt // 4])
                        for n in range(NCH):
                            li = t_.matmul(psA[:, n * 512:(n + 1) * 512],
                                      xnT[:, jt, st * 128:(st + 1) * 128],
                                      wv_s[:, jt, n * 512:(n + 1) * 512],
                                      start=(jt == 0), stop=False)
                    for n in range(NCH):
                        li = t_.matmul(psA[:, n * 512:(n + 1) * 512],
                                  onesb[0:1, :], vb_s[0:1, n * 512:(n + 1) * 512],
                                  start=False, stop=True)
                    tok = dep.inc(li, "t")
                    dep.did_w(tok, b_psA)
                    dep.did_r(tok, b_xnT)
                    for c in range(4):
                        dep.did_r(tok, b_wv[c])

                T(vproj)

                def vev(s, st=st, sl=sl):
                    dep.rd(s, "s", b_psA)
                    dep.wr(s, "s", b_vst[sl])
                    tok = dep.inc(s.copy(vst[sl][:, :], psA[:, :]), "s")
                    dep.did_w(tok, b_vst[sl])
                    dep.did_r(tok, b_psA)

                S(vev)

                def vpub(s, st=st, sl=sl):
                    dep.rd(s, "s", b_vst[sl])
                    off = st * 128 * D
                    tok = dep.inc(s.dma_start(
                        out=kv_in[off:off + 128 * D]
                        .rearrange("(p n) -> p n", p=128),
                        in_=vst[sl][:, :]), "sp%d" % sl, 16)
                    dep.did_r(tok, b_vst[sl], b_kvin)

                S(vpub)

            dep.guard_merge(*b_wv, b_vst[0], b_vst[1], b_psA, b_xnt[0],
                            b_xnt[1])

        b_kvoutv = Buf()
        b_kvout = Buf()

        def ag_v(g):
            for t in list(b_kvin.rs.values()):
                dep.wait(g, "g", t)
            tok = dep.inc(g.collective_compute(
                "AllGather", ALU.bypass,
                replica_groups=[list(range(NCORE))],
                ins=[kv_in[0:KVN].opt()],
                outs=[v_out[:].opt()]), "cc")
            dep.did_w(tok, b_kvoutv)

        Gp(ag_v)

        qT = ctx.enter_context(nc.sbuf_tensor("qT", [128, NG * REG], BF16,
                                              side="right"))

        with ExitStack() as c2:
            csq = c2.enter_context(nc.sbuf_tensor("csq", [128, S_CORE], F32))
            snq = c2.enter_context(nc.sbuf_tensor("snq", [128, S_CORE], F32))
            wcol = [c2.enter_context(
                nc.sbuf_tensor(f"wcol{i}", [128, ET, 128], BF16))
                for i in range(2)]
            tq = [c2.enter_context(nc.sbuf_tensor(f"tq{i}", [128, S_CORE],
                                                  BF16)) for i in range(2)]
            m1 = c2.enter_context(nc.sbuf_tensor("m1", [128, S_CORE], BF16))
            m2 = c2.enter_context(nc.sbuf_tensor("m2", [128, S_CORE], BF16))
            kst = [c2.enter_context(nc.sbuf_tensor(f"kst{i}", [128, S_CORE],
                                                   BF16)) for i in range(2)]
            b_tab = dep.newbuf()
            b_wcol = [dep.newbuf(), dep.newbuf()]
            b_tq = [dep.newbuf(), dep.newbuf()]
            b_m = dep.newbuf()
            b_kst = [dep.newbuf(), dep.newbuf()]

            def tabs(q, ch, sh):
                dep.wr(q, "q", b_tab)
                dep.inc(q.dma_start(out=csq[:, :], in_=ch[:, :]), "qt", 16)
                t = dep.inc(q.dma_start(out=snq[:, :], in_=sh[:, :]),
                            "qt", 16)
                dep.did_w(t, b_tab)

            Q(lambda q: tabs(q, cosk_h, sink_h))

            def wload(g, it, sl, wh):
                dep.wr(g, "g", b_wcol[sl])
                t = dep.inc(g.dma_start(
                    out=wcol[sl][:, :, :],
                    in_=wh[it, :, :, :]), "gw%d" % sl, 16)
                dep.did_w(t, b_wcol[sl])

            Gp(lambda g: wload(g, 0, 0, wk_h))

            W1 = min(512, S_CORE)
            NSC = S_CORE // W1
            for it in range(2 * ET):
                isq = it >= ET
                et = it % ET
                sl = it % 2
                if it == ET:
                    Q(lambda q: tabs(q, cosq_h, sinq_h))

                    def ag_k(g):
                        for t in list(b_kvin.rs.values()):
                            dep.wait(g, "g", t)
                        tok = dep.inc(g.collective_compute(
                            "AllGather", ALU.bypass,
                            replica_groups=[list(range(NCORE))],
                            ins=[kv_in[KVN:2 * KVN].opt()],
                            outs=[k_out[:].opt()]), "cc")
                        dep.did_w(tok, b_kvout)

                    Gp(ag_k)
                if it + 1 < 2 * ET:
                    nxt = it + 1
                    Gp(lambda g, nxt=nxt, sl=sl: wload(
                        g, nxt % ET, 1 - sl, wk_h if nxt < ET else wq_h))

                pt = psA if it % 2 == 0 else psB
                pbufs = ([b_psA] if it % 2 == 0 else
                         [b_psB[0], b_psB[1], b_psRS, b_psSp])

                def proj(t_, it=it, sl=sl, pt=pt, pbufs=pbufs):
                    dep.rd(t_, "t", b_wcol[sl], b_xnT)
                    for _b in pbufs:
                        dep.wr(t_, "t", _b)
                    for jt in range(ET):
                        for n in range(NSC):
                            li = t_.matmul(pt[:, n * W1:(n + 1) * W1],
                                      wcol[sl][:, jt, :],
                                      xnT[:, jt, n * W1:(n + 1) * W1],
                                      start=(jt == 0), stop=(jt == ET - 1))
                    tok = dep.inc(li, "t")
                    dep.did_w(tok, *pbufs)
                    dep.did_r(tok, b_wcol[sl], b_xnT)

                T(proj)

                def bias(s, it=it, isq=isq, et=et, pt=pt, pbufs=pbufs,
                         sl=sl):
                    for _b in pbufs:
                        dep.rd(s, "s", _b)
                    dep.wr(s, "s", b_tq[sl])
                    bb = qb_s if isq else kb_s
                    tok = dep.inc(s.activation(tq[sl][:, :], pt[:, 0:S_CORE],
                                               AF.Identity,
                                               bias=bb[:, et:et + 1]), "s")
                    dep.did_w(tok, b_tq[sl])
                    for _b in pbufs:
                        dep.did_r(tok, _b)

                S(bias)

                def rot(t_, it=it, pt=pt, pbufs=pbufs, sl=sl):
                    dep.rd(t_, "t", b_tq[sl], b_consts)
                    for _b in pbufs:
                        dep.wr(t_, "t", _b)
                    for n in range(NSC):
                        li = t_.matmul(pt[:, n * W1:(n + 1) * W1], rotm[:, :],
                                  tq[sl][:, n * W1:(n + 1) * W1],
                                  start=True, stop=True)
                    tok = dep.inc(li, "t")
                    dep.did_w(tok, *pbufs)
                    dep.did_r(tok, b_tq[sl])

                T(rot)

                def rope(v, it=it, isq=isq, et=et, sl=sl, pt=pt, pbufs=pbufs):
                    cs, sn = csq, snq
                    dep.rd(v, "v", b_tq[sl], b_tab)
                    for _b in pbufs:
                        dep.rd(v, "v", _b)
                    dep.wr(v, "v", b_m)
                    v.tensor_mul(m1[:, :], tq[sl][:, :], cs[:, :])
                    tok = dep.inc(v.tensor_mul(m2[:, :], pt[:, 0:S_CORE],
                                               sn[:, :]), "v")
                    dep.did_w(tok, b_m)
                    dep.did_r(tok, b_tq[sl], b_tab)
                    for _b in pbufs:
                        dep.did_r(tok, _b)
                    v.wait_ge(dep.sem["v"], tok[1])
                    if isq:
                        for g in range(NG):
                            dep.wr(v, "v", b_qTg[g])
                        dst = qT[:, :].rearrange("p (g r) -> p g r", g=NG)[
                            :, :, et * SW:(et + 1) * SW]
                        src1 = m1[:, :].rearrange("p (g r) -> p g r", g=NG)
                        src2 = m2[:, :].rearrange("p (g r) -> p g r", g=NG)
                        tok = dep.inc(v.tensor_add(dst, src1, src2), "v")
                        for g in range(NG):
                            dep.did_w(tok, b_qTg[g])
                    else:
                        dep.wr(v, "v", b_kst[sl])
                        tok = dep.inc(v.tensor_add(kst[sl][:, :], m1[:, :],
                                                   m2[:, :]), "v")
                        dep.did_w(tok, b_kst[sl])
                    dep.did_r(tok, b_m)

                V(rope)

                if not isq:
                    def kpub(s, et=et, sl=sl):
                        dep.rd(s, "s", b_kst[sl])
                        off = KVN + et * 128 * S_CORE
                        tok = dep.inc(s.dma_start(
                            out=kv_in[off:off + 128 * S_CORE]
                            .rearrange("(p n) -> p n", p=128),
                            in_=kst[sl][:, :]), "kp%d" % sl, 16)
                        dep.did_r(tok, b_kst[sl], b_kvin)

                    S(kpub)

            dep.guard_merge(b_tab, b_wcol[0], b_wcol[1], b_tq[0], b_tq[1],
                            b_m, b_kst[0], b_kst[1])
        dep.guard_merge(b_xnT)
        c12.close()


        wo_c0 = ctx.enter_context(nc.sbuf_tensor("wo_c0", [128, ET, 512],
                                                 BF16, side="right"))

        with ExitStack() as c3:
            kbuf = [c3.enter_context(nc.sbuf_tensor(f"kbuf{i}", [128, ET, TP],
                                                    BF16)) for i in range(2)]
            vbuf = [c3.enter_context(nc.sbuf_tensor(f"vbuf{i}",
                                                    [128, T_SUP, D], BF16))
                    for i in range(2)]
            expT = [c3.enter_context(nc.sbuf_tensor(f"expT{i}",
                                                    [128, T_SUP, SW], BF16))
                    for i in range(2)]
            o_acc = c3.enter_context(nc.sbuf_tensor("o_acc",
                                                    [128, G_ST, D], F32))
            rrow = c3.enter_context(nc.sbuf_tensor("rrow", [1, SW], F32))
            rrec = c3.enter_context(nc.sbuf_tensor("rrec", [1, SW], F32))
            rct = c3.enter_context(nc.sbuf_tensor("rct", [128, G_ST], F32))
            def _tmax(a, b):
                if a is None:
                    return b
                if b is None:
                    return a
                return a if a[1] >= b[1] else b

            b_psAh[0].w = _tmax(b_psAh[0].w, b_psA.w)
            b_psAh[1].w = _tmax(b_psAh[1].w, b_psA.w)
            for _k, _t in b_psA.rs.items():
                b_psAh[0].rs[_k] = _t
                b_psAh[1].rs[_k] = _t
            b_kb = [dep.newbuf(), dep.newbuf()]
            b_vb = [dep.newbuf(), dep.newbuf()]
            b_ex = [dep.newbuf(), dep.newbuf()]
            b_oac = [dep.newbuf() for _ in range(G_ST)]
            b_r = dep.newbuf()
            b_rct = dep.newbuf()

            def kload(q, su_idx):
                su = su_idx % NSUP
                ks = su_idx % 2
                c = (su * TP) // S_CORE
                r0 = (su * TP) % S_CORE
                base = c * KVN
                dep.rd(q, "q", b_kvout)
                dep.wr(q, "q", b_kb[ks])
                tok = dep.inc(q.dma_start(
                    out=kbuf[ks][:, :, :],
                    in_=k_out[base:base + KVN]
                    .rearrange("(a p s) -> p a s", p=128, s=S_CORE)
                    [:, :, r0:r0 + TP]), "qk%d" % ks, 16)
                dep.did_w(tok, b_kb[ks])

            def vload(q, su_idx):
                su = su_idx % NSUP
                vs = su_idx % 2
                c = (su * TP) // S_CORE
                r0 = (su * TP) % S_CORE
                base = c * KVN
                dep.rd(q, "q", b_kvoutv)
                dep.wr(q, "q", b_vb[vs])
                vb0 = base + r0 * D
                tok = dep.inc(q.dma_start(
                    out=vbuf[vs][:, :, :],
                    in_=v_out[vb0:vb0 + TP * D]
                    .rearrange("(a p e) -> p a e", p=128, e=D)), "qv%d" % vs,
                    16)
                dep.did_w(tok, b_vb[vs])

            Q(lambda q: kload(q, 0))
            Q(lambda q: vload(q, 0))

            def wo_c0_load(g):
                g.wait_ge(dep.sem["gb"], dep.cnt["gb"])
                tok = dep.inc(g.dma_start(
                    out=wo_c0[:, :, :],
                    in_=wo_h[:, :, 0:512].rearrange("a p n -> p a n")),
                    "gb", 16)
                dep.did_w(tok, b_wo_c0)

            Gp(wo_c0_load)

            def scores_block(g, ssu, su_idx):
                ks = su_idx % 2
                ex = su_idx % 2
                for tt in range(T_SUP):
                    pb = tt % 2

                    def sc(t_, g=g, tt=tt, ks=ks, pb=pb):
                        dep.rd(t_, "t", b_kb[ks], b_qTg[g])
                        dep.wr(t_, "t", b_psB[pb])
                        for et in range(ET):
                            li = t_.matmul(psB[:, pb * 512:pb * 512 + SW],
                                           kbuf[ks][:, et,
                                                    tt * 128:(tt + 1) * 128],
                                           qT[:, g * REG + et * SW:
                                              g * REG + (et + 1) * SW],
                                           start=(et == 0),
                                           stop=(et == ET - 1))
                        tok = dep.inc(li, "t")
                        dep.did_w(tok, b_psB[pb])
                        dep.did_r(tok, b_kb[ks], b_qTg[g])

                    T(sc)

                    def ex_(s, tt=tt, ex=ex, pb=pb):
                        dep.rd(s, "s", b_psB[pb])
                        dep.wr(s, "s", b_ex[ex])
                        tok = dep.inc(s.activation(
                            expT[ex][:, tt, :],
                            psB[:, pb * 512:pb * 512 + SW], AF.Exp), "s")
                        b_ex[ex].w = tok
                        dep.did_r(tok, b_psB[pb])

                    S(ex_)

            def pv_block(g, ssu, su_idx):
                ex = su_idx % 2
                vs = su_idx % 2

                def rsum(t_, g=g, ssu=ssu, ex=ex):
                    dep.rd(t_, "t", b_ex[ex], b_consts)
                    if ssu == 0:
                        dep.wr(t_, "t", b_psRS)
                    for tt in range(T_SUP):
                        li = t_.matmul(psB[0:1, 1024:1024 + SW],
                                       onesb[:, 0:1], expT[ex][:, tt, :],
                                       start=(ssu == 0 and tt == 0),
                                       stop=(ssu == NSUP - 1 and
                                             tt == T_SUP - 1))
                    tok = dep.inc(li, "t")
                    dep.did_w(tok, b_psRS)
                    dep.did_r(tok, b_ex[ex])

                T(rsum)

                for stl in range(G_ST):
                    def pv(t_, ssu=ssu, vs=vs, ex=ex, stl=stl):
                        dep.rd(t_, "t", b_ex[ex], b_vb[vs])
                        for h in range(2):
                            dep.wr(t_, "t", b_psAh[h])
                            for tt in range(T_SUP):
                                for n in (0, 1):
                                    nn = h * 2 + n
                                    li = t_.matmul(
                                        psA[:, nn * 512:(nn + 1) * 512],
                                        expT[ex][:, tt,
                                                 stl * 128:(stl + 1) * 128],
                                        vbuf[vs][:, tt,
                                                 nn * 512:(nn + 1) * 512],
                                        start=(tt == 0),
                                        stop=(tt == T_SUP - 1))
                            tok = dep.inc(li, "t")
                            dep.did_w(tok, b_psAh[h])
                            dep.did_r(tok, b_ex[ex], b_vb[vs])

                    T(pv)

                    def fl(v, ssu=ssu, stl=stl):
                        for h in range(2):
                            dep.rd(v, "v", b_psAh[h])
                            dep.wr(v, "v", b_oac[stl])
                            dst = o_acc[:, stl, h * 1024:(h + 1) * 1024]
                            if ssu == 0:
                                ins = v.tensor_copy(
                                    dst, psA[:, h * 1024:(h + 1) * 1024])
                            else:
                                ins = v.tensor_add(
                                    dst, dst,
                                    psA[:, h * 1024:(h + 1) * 1024])
                            tok = dep.inc(ins, "v")
                            dep.did_w(tok, b_oac[stl])
                            dep.did_r(tok, b_psAh[h])

                    V(fl)

            def sweep_end(g):
                def rfin_s(s, g=g):
                    dep.rd(s, "s", b_psRS)
                    dep.wr(s, "s", b_r)
                    tok = dep.inc(s.copy(rrow[0:1, :],
                                         psB[0:1, 1024:1024 + SW]), "s")
                    dep.did_w(tok, b_r)
                    dep.did_r(tok, b_psRS)

                S(rfin_s)

                def rfin_v(v, g=g):
                    dep.rd(v, "v", b_r)
                    tok = dep.inc(v.reciprocal(rrec[0:1, :], rrow[0:1, :]),
                                  "v")
                    dep.did_w(tok, b_r)

                V(rfin_v)

                def rtr(t_, g=g):
                    dep.rd(t_, "t", b_r, b_consts)
                    dep.wr(t_, "t", b_psSp)
                    for stl in range(G_ST):
                        li = t_.matmul(psB[:, 1536 + stl:1536 + stl + 1],
                                       rrec[0:1, stl * 128:(stl + 1) * 128],
                                       onesf[0:1, 0:1], start=True, stop=True)
                    tok = dep.inc(li, "t")
                    dep.did_w(tok, b_psSp)
                    dep.did_r(tok, b_r)

                T(rtr)

                def rev(s, g=g):
                    dep.rd(s, "s", b_psSp)
                    dep.wr(s, "s", b_rct)
                    tok = dep.inc(s.copy(rct[:, :],
                                         psB[:, 1536:1536 + G_ST]), "s")
                    dep.did_w(tok, b_rct)
                    dep.did_r(tok, b_psSp)

                S(rev)

                for stl in range(G_ST):
                    def obf(v, g=g, stl=stl):
                        dep.rd(v, "v", b_oac[stl], b_rct)
                        dep.wr(v, "v", b_qTg[g])
                        dst = qT[:, g * REG + stl * D:g * REG + (stl + 1) * D]
                        tok = dep.inc(v.tensor_scalar(
                            dst, o_acc[:, stl, :], rct[:, stl:stl + 1], None,
                            op0=ALU.mult), "v")
                        b_qTg[g].w = tok
                        dep.did_r(tok, b_oac[stl], b_rct)

                    V(obf)

            NSU_ALL = NG * NSUP
            for su_idx in range(NSU_ALL + 1):
                if su_idx < NSU_ALL:
                    g, ssu = divmod(su_idx, NSUP)
                    if su_idx + 1 < NSU_ALL:
                        Q(lambda q, si=su_idx + 1: kload(q, si))
                    scores_block(g, ssu, su_idx)
                if su_idx >= 1:
                    pg, pssu = divmod(su_idx - 1, NSUP)
                    pv_block(pg, pssu, su_idx - 1)
                    if pssu == NSUP - 1:
                        sweep_end(pg)
                if 1 <= su_idx + 1 < NSU_ALL:
                    Q(lambda q, si=su_idx + 1: vload(q, si))

            dep.guard_merge(b_kb[0], b_kb[1], b_vb[0], b_vb[1], b_ex[0],
                            b_ex[1], *b_oac, b_r, b_rct)

        def _tmax2(a, b):
            if a is None:
                return b
            if b is None:
                return a
            return a if a[1] >= b[1] else b

        b_psA.w = _tmax2(_tmax2(b_psA.w, b_psAh[0].w), b_psAh[1].w)
        for _h in (0, 1):
            for _k, _t in b_psAh[_h].rs.items():
                if _k not in b_psA.rs or b_psA.rs[_k][1] < _t[1]:
                    b_psA.rs[_k] = _t

        with ExitStack() as c4:
            wo_s = c4.enter_context(nc.sbuf_tensor("wo_s", [128, ET, 3 * 512],
                                                   BF16))
            rw_s = c4.enter_context(nc.sbuf_tensor("rw_s", [128, ET, NEXP],
                                                   BF16))
            rb_s = c4.enter_context(nc.sbuf_tensor("rb_s", [1, NEXP], BF16))
            eb_s = c4.enter_context(nc.sbuf_tensor("eb_s", [NEXP, D], BF16))
            scr = c4.enter_context(nc.sbuf_tensor("scr2", [128, D], F32))
            b_scr = dep.newbuf()
            oT = [c4.enter_context(nc.sbuf_tensor(f"oT{i}", [128, ET, 128],
                                                  BF16)) for i in range(2)]
            x2T = c4.enter_context(nc.sbuf_tensor("x2T", [128, ET, 128],
                                                  BF16))
            x1 = [c4.enter_context(nc.sbuf_tensor(f"x1_{i}", [128, D], F32))
                  for i in range(2)]
            xn2 = [c4.enter_context(nc.sbuf_tensor(f"xn2_{i}", [128, D],
                                                   BF16)) for i in range(2)]
            x2 = c4.enter_context(nc.sbuf_tensor("x2_", [128, D], F32))
            pp = c4.enter_context(nc.sbuf_tensor("pp", [128, NEXP], F32))
            wk64 = c4.enter_context(nc.sbuf_tensor("wk64", [128, NEXP], F32))
            top8 = c4.enter_context(nc.sbuf_tensor("top8", [128, TOPK], F32))
            msk = c4.enter_context(nc.sbuf_tensor("msk", [128, NEXP], F32))
            mpb = c4.enter_context(nc.sbuf_tensor("mpb", [128, NEXP], BF16))
            mpT = c4.enter_context(nc.sbuf_tensor("mpT", [64, 128], BF16))
            b_wo = [dep.newbuf() for _ in range(NCH)]
            b_rwc = dep.newbuf()
            b_oT = [dep.newbuf(), dep.newbuf()]
            b_x2T = dep.newbuf()
            b_x1 = [dep.newbuf(), dep.newbuf()]
            b_xn2 = [dep.newbuf(), dep.newbuf()]
            b_x2 = dep.newbuf()
            b_p = dep.newbuf()
            b_mp = dep.newbuf()
            b_mpT = dep.newbuf()
            b_lnp = [b_ln, dep.newbuf()]

            def wo_load(g):
                dep.wr(g, "g", b_rwc)
                g.wait_ge(dep.sem["gb"], dep.cnt["gb"])
                dep.inc(g.dma_start(
                    out=rw_s[:, :, :],
                    in_=rw_h[:, :, :].rearrange("a p n -> p a n")), "gb", 16)
                g.wait_ge(dep.sem["gb"], dep.cnt["gb"])
                dep.inc(g.dma_start(out=rb_s[:, :], in_=rb_h[:, :]), "gb", 16)
                g.wait_ge(dep.sem["gb"], dep.cnt["gb"])
                tok = dep.inc(g.dma_start(out=eb_s[:, :], in_=eb_h[:, :]),
                              "gb", 16)
                dep.did_w(tok, b_rwc)
                for n in range(1, NCH):
                    dep.wr(g, "g", b_wo[n])
                    g.wait_ge(dep.sem["gb"], dep.cnt["gb"])
                    tok = dep.inc(g.dma_start(
                        out=wo_s[:, :, (n - 1) * 512:n * 512],
                        in_=wo_h[:, :, n * 512:(n + 1) * 512]
                        .rearrange("a p n -> p a n")), "gb", 16)
                    dep.did_w(tok, b_wo[n])

            Gp(wo_load)
            Q(lambda q: xload(q, 0, 0))

            def emit_otr(st):
                sl = st % 2
                g = st // G_ST
                stl = st % G_ST
                for b4 in range(4):
                    slot = b4 % 2

                    def otr(t_, g=g, stl=stl, b4=b4, slot=slot):
                        dep.rd(t_, "t", b_qTg[g], b_consts)
                        dep.wr(t_, "t", b_psB[slot])
                        ob = qT[:, g * REG + stl * D:g * REG + (stl + 1) * D]
                        for b in range(4):
                            eb_ = b4 * 4 + b
                            li = t_.matmul(psB[:, slot * 512 + b * 128:
                                               slot * 512 + (b + 1) * 128],
                                           ob[:, eb_ * 128:(eb_ + 1) * 128],
                                           idb[:, :], start=True, stop=True)
                        tok = dep.inc(li, "t")
                        dep.did_w(tok, b_psB[slot])
                        dep.did_r(tok, b_qTg[g])

                    T(otr)

                    def oev(s, st=st, b4=b4, slot=slot, sl=sl):
                        dep.rd(s, "s", b_psB[slot])
                        dep.wr(s, "s", b_oT[sl])
                        tok = dep.inc(s.copy(
                            oT[sl][:, b4 * 4:(b4 + 1) * 4, :],
                            psB[:, slot * 512:(slot + 1) * 512]
                            .rearrange("p (a n) -> p a n", n=128)), "s")
                        b_oT[sl].w = tok
                        dep.did_r(tok, b_psB[slot])

                    S(oev)

            def emit_yproj(st):
                sl = st % 2

                def yproj(t_, sl=sl):
                    dep.rd(t_, "t", b_oT[sl])
                    dep.wr(t_, "t", b_psA)
                    for n in range(NCH):
                        if n == 0:
                            dep.rd(t_, "t", b_wo_c0)
                        else:
                            dep.rd(t_, "t", b_wo[n])
                        for et in range(ET):
                            src = (wo_c0[:, et, :] if n == 0 else
                                   wo_s[:, et, (n - 1) * 512:n * 512])
                            li = t_.matmul(psA[:, n * 512:(n + 1) * 512],
                                           oT[sl][:, et, :], src,
                                           start=(et == 0),
                                           stop=(et == ET - 1))
                    tok = dep.inc(li, "t")
                    dep.did_w(tok, b_psA)
                    dep.did_r(tok, b_oT[sl], b_wo_c0)
                    for n in range(1, NCH):
                        dep.did_r(tok, b_wo[n])

                T(yproj)

            emit_otr(0)
            emit_yproj(0)

            for st in range(NST):
                sl = st % 2
                g = st // G_ST
                stl = st % G_ST
                ln_o = 16 * sl
                b_lnc = b_lnp[sl]
                if st + 1 < NST:
                    Q(lambda q, st=st: xload(q, st + 1, (st + 1) % 2))

                def resid(v, sl=sl, ln_o=ln_o, b_lnc=b_lnc):
                    dep.rd(v, "v", b_psA, b_xt[sl])
                    dep.wr(v, "v", b_x1[sl], b_lnc)
                    tok = dep.inc(v.scalar_tensor_tensor(
                        x1[sl][:, :], psA[:, :], 1.0, xt[sl][:, :],
                        op0=ALU.mult, op1=ALU.add,
                        accum_out=ln[:, ln_o + 0:ln_o + 1]), "v")
                    dep.did_w(tok, b_x1[sl])
                    b_lnc.w = tok
                    dep.did_r(tok, b_psA, b_xt[sl])

                V(resid)

                def l1(v, sl=sl, ln_o=ln_o, b_lnc=b_lnc):
                    dep.rd(v, "v", b_x1[sl], b_lnc)
                    dep.wr(v, "v", b_scr)
                    t = dep.inc(v.scalar_tensor_tensor(
                        scr[:, :], x1[sl][:, :], 1.0, x1[sl][:, :],
                        op0=ALU.mult, op1=ALU.mult,
                        accum_out=ln[:, ln_o + 1:ln_o + 2]), "v")
                    dep.did_w(t, b_scr)
                    dep.did_r(t, b_x1[sl])
                    v.wait_ge(dep.sem["v"], dep.cnt["v"])
                    v.tensor_scalar(ln[:, ln_o + 2:ln_o + 3],
                                    ln[:, ln_o + 0:ln_o + 1], 1.0 / D, None,
                                    op0=ALU.mult)
                    dep.inc(v.tensor_scalar(ln[:, ln_o + 3:ln_o + 4],
                                            ln[:, ln_o + 1:ln_o + 2],
                                            -1.0 / D, -EPS,
                                            op0=ALU.mult, op1=ALU.add), "v")
                    v.wait_ge(dep.sem["v"], dep.cnt["v"])
                    t = dep.inc(v.scalar_tensor_tensor(
                        ln[:, ln_o + 4:ln_o + 5], ln[:, ln_o + 2:ln_o + 3],
                        ln[:, ln_o + 2:ln_o + 3], ln[:, ln_o + 3:ln_o + 4],
                        op0=ALU.mult, op1=ALU.add), "v")
                    dep.did_w(t, b_lnc)

                V(l1)

                def l2(s, ln_o=ln_o, b_lnc=b_lnc):
                    dep.rd(s, "s", b_lnc)
                    t = dep.inc(s.activation(ln[:, ln_o + 5:ln_o + 6],
                                             ln[:, ln_o + 4:ln_o + 5],
                                             AF.Ln, scale=-1.0), "s")
                    s.wait_ge(dep.sem["s"], t[1])
                    t = dep.inc(s.activation(ln[:, ln_o + 6:ln_o + 7],
                                             ln[:, ln_o + 5:ln_o + 6],
                                             AF.Exp, scale=-0.5), "s")
                    dep.did_w(t, b_lnc)

                S(l2)

                def l3(v, ln_o=ln_o, b_lnc=b_lnc):
                    dep.rd(v, "v", b_lnc)
                    t = dep.inc(v.scalar_tensor_tensor(
                        ln[:, ln_o + 7:ln_o + 8], ln[:, ln_o + 2:ln_o + 3],
                        -1.0, ln[:, ln_o + 6:ln_o + 7],
                        op0=ALU.mult, op1=ALU.mult), "v")
                    dep.did_w(t, b_lnc)

                V(l3)

                def l4(s, sl=sl, ln_o=ln_o, b_lnc=b_lnc):
                    dep.rd(s, "s", b_lnc, b_x1[sl])
                    dep.wr(s, "s", b_xn2[sl])
                    t = dep.inc(s.activation(xn2[sl][:, :], x1[sl][:, :],
                                             AF.Identity,
                                             bias=ln[:, ln_o + 7:ln_o + 8],
                                             scale=ln[:, ln_o + 6:ln_o + 7]),
                                "s")
                    dep.did_w(t, b_xn2[sl])
                    dep.did_r(t, b_x1[sl], b_lnc)

                S(l4)

                for b4 in range(4):
                    slot = b4 % 2

                    def xtr(t_, sl=sl, b4=b4, slot=slot):
                        dep.rd(t_, "t", b_xn2[sl], b_consts)
                        dep.wr(t_, "t", b_psB[slot])
                        for b in range(4):
                            jt = b4 * 4 + b
                            li = t_.matmul(psB[:, slot * 512 + b * 128:
                                               slot * 512 + (b + 1) * 128],
                                           xn2[sl][:, jt * 128:(jt + 1) * 128],
                                           idb[:, :], start=True, stop=True)
                        tok = dep.inc(li, "t")
                        dep.did_w(tok, b_psB[slot])
                        dep.did_r(tok, b_xn2[sl])

                    T(xtr)

                    def xev(s, b4=b4, slot=slot):
                        dep.rd(s, "s", b_psB[slot])
                        dep.wr(s, "s", b_x2T)
                        tok = dep.inc(s.copy(
                            x2T[:, b4 * 4:(b4 + 1) * 4, :],
                            psB[:, slot * 512:(slot + 1) * 512]
                            .rearrange("p (a n) -> p a n", n=128)), "s")
                        b_x2T.w = tok
                        dep.did_r(tok, b_psB[slot])

                    S(xev)

                def logits(t_):
                    dep.rd(t_, "t", b_x2T, b_consts, b_rwc)
                    dep.wr(t_, "t", b_psSp)
                    for jt in range(ET):
                        li = t_.matmul(psB[:, 1536:1536 + NEXP],
                                       x2T[:, jt, :], rw_s[:, jt, :],
                                       start=(jt == 0), stop=False)
                    li = t_.matmul(psB[:, 1536:1536 + NEXP], onesb[0:1, :],
                                   rb_s[0:1, :], start=False, stop=True)
                    tok = dep.inc(li, "t")
                    dep.did_w(tok, b_psSp)
                    dep.did_r(tok, b_x2T)

                T(logits)

                def smax_v(v, ln_o=ln_o, b_lnc=b_lnc):
                    dep.rd(v, "v", b_psSp)
                    dep.wr(v, "v", b_lnc)
                    t0 = dep.inc(v.reduce_max(ln[:, ln_o + 8:ln_o + 9],
                                              psB[:, 1536:1536 + NEXP],
                                              axis=AX.X), "v")
                    v.wait_ge(dep.sem["v"], t0[1])
                    t = dep.inc(v.tensor_scalar(ln[:, ln_o + 9:ln_o + 10],
                                                ln[:, ln_o + 8:ln_o + 9],
                                                -1.0, None, op0=ALU.mult),
                                "v")
                    dep.did_w(t, b_lnc)
                    dep.did_r(t, b_psSp)

                V(smax_v)

                def smax_s(s, ln_o=ln_o, b_lnc=b_lnc):
                    dep.rd(s, "s", b_lnc, b_psSp)
                    dep.wr(s, "s", b_p)
                    t = dep.inc(s.activation(pp[:, :],
                                             psB[:, 1536:1536 + NEXP],
                                             AF.Exp,
                                             bias=ln[:, ln_o + 9:ln_o + 10],
                                             accum_out=ln[:, ln_o + 10:
                                                          ln_o + 11]), "s")
                    dep.did_w(t, b_p)
                    b_lnc.w = t
                    dep.did_r(t, b_psSp)

                S(smax_s)

                def topk(v, ln_o=ln_o, b_lnc=b_lnc):
                    dep.rd(v, "v", b_p, b_lnc)
                    dep.wr(v, "v", b_mp)
                    v.reciprocal(ln[:, ln_o + 11:ln_o + 12],
                                 ln[:, ln_o + 10:ln_o + 11])
                    v.max(top8[:, :], pp[:, :])
                    t = dep.inc(v.engine_nop(), "v")
                    v.wait_ge(dep.sem["v"], t[1])
                    v.match_replace(wk64[:, :], top8[:, :], pp[:, :], -1.0)
                    t = dep.inc(v.engine_nop(), "v")
                    v.wait_ge(dep.sem["v"], t[1])
                    v.tensor_tensor(msk[:, :], wk64[:, :], pp[:, :],
                                    op=ALU.not_equal)
                    t = dep.inc(v.engine_nop(), "v")
                    v.wait_ge(dep.sem["v"], t[1])
                    v.tensor_mul(mpb[:, :], msk[:, :], pp[:, :])
                    t = dep.inc(v.reduce_sum(ln[:, ln_o + 12:ln_o + 13],
                                             top8[:, :], axis=AX.X), "v")
                    b_lnc.w = t
                    dep.did_w(t, b_mp)
                    dep.did_r(t, b_p)

                V(topk)

                if st + 1 < NST:
                    emit_otr(st + 1)
                    emit_yproj(st + 1)

                def mptr(t_):
                    dep.rd(t_, "t", b_mp, b_consts)
                    dep.wr(t_, "t", b_psRS)
                    li = t_.matmul(psB[0:64, 1024:1024 + 128], mpb[:, :],
                                   idb[:, :], start=True, stop=True)
                    tok = dep.inc(li, "t")
                    dep.did_w(tok, b_psRS)
                    dep.did_r(tok, b_mp)

                T(mptr)

                def mpev(s):
                    dep.rd(s, "s", b_psRS)
                    dep.wr(s, "s", b_mpT)
                    tok = dep.inc(s.copy(mpT[0:64, :],
                                         psB[0:64, 1024:1024 + 128]), "s")
                    dep.did_w(tok, b_mpT)
                    dep.did_r(tok, b_psRS)

                S(mpev)

                for n in range(NCH):
                    bb = b_psRS if n % 2 == 0 else b_psSp
                    boff = 1024 if n % 2 == 0 else 1536

                    def bmat(t_, n=n, bb=bb, boff=boff):
                        dep.rd(t_, "t", b_mpT, b_rwc)
                        dep.wr(t_, "t", bb)
                        li = t_.matmul(psB[:, boff:boff + 512], mpT[0:64, :],
                                       eb_s[0:64, n * 512:(n + 1) * 512],
                                       start=True, stop=True)
                        tok = dep.inc(li, "t")
                        dep.did_w(tok, bb)
                        dep.did_r(tok, b_mpT)

                    T(bmat)

                    def ffn_c(v, sl=sl, n=n, bb=bb, boff=boff, ln_o=ln_o,
                              b_lnc=b_lnc):
                        dep.rd(v, "v", bb, b_xn2[sl], b_lnc)
                        dep.wr(v, "v", b_xt[sl])
                        tok = dep.inc(v.scalar_tensor_tensor(
                            xt[sl][:, n * 512:(n + 1) * 512],
                            xn2[sl][:, n * 512:(n + 1) * 512],
                            ln[:, ln_o + 12:ln_o + 13],
                            psB[:, boff:boff + 512],
                            op0=ALU.mult, op1=ALU.add), "v")
                        b_xt[sl].w = tok
                        dep.did_r(tok, bb, b_xn2[sl])

                    V(ffn_c)

                def ffn_out(v, sl=sl, ln_o=ln_o, b_lnc=b_lnc):
                    dep.rd(v, "v", b_xt[sl], b_x1[sl], b_lnc)
                    dep.wr(v, "v", b_x2)
                    tok = dep.inc(v.scalar_tensor_tensor(
                        x2[:, :], xt[sl][:, :], ln[:, ln_o + 11:ln_o + 12],
                        x1[sl][:, :], op0=ALU.mult, op1=ALU.add), "v")
                    dep.did_w(tok, b_x2)
                    dep.did_r(tok, b_xt[sl], b_x1[sl], b_lnc)

                V(ffn_out)

                def ywrite(q, st=st):
                    dep.rd(q, "q", b_x2)
                    tok = dep.inc(q.dma_start(
                        out=y_h[st * 128:(st + 1) * 128, :],
                        in_=x2[:, :]), "yw0", 16)
                    dep.did_r(tok, b_x2)

                Q(ywrite)

            def fin(q):
                q.wait_ge(dep.sem["yw0"], dep.cnt["yw0"])

            Q(fin)

    return nc


def _host_prep(x, qw, qb, kvw, kvb, ow, rw, rb, eb, S_CORE):
    S_TOT = NCORE * S_CORE
    f32 = np.float32
    bf16 = ml_dtypes.bfloat16

    xs = np.ascontiguousarray(np.asarray(x, dtype=f32).reshape(S_TOT, D))
    wq_pack = np.ascontiguousarray(
        np.asarray(qw, f32).T.reshape(ET, 128, ET, 128).transpose(2, 1, 0, 3))
    kw = np.asarray(kvw, f32)[:D, :]
    vw = np.asarray(kvw, f32)[D:, :]
    wk_pack = np.ascontiguousarray(
        kw.T.reshape(ET, 128, ET, 128).transpose(2, 1, 0, 3))
    wv_mov = np.ascontiguousarray(vw.T).reshape(ET, 128, D)
    wo_mov = np.ascontiguousarray(np.asarray(ow, f32).T).reshape(ET, 128, D)
    qb_t = np.ascontiguousarray(np.asarray(qb, f32).reshape(ET, 128).T)
    kb_t = np.ascontiguousarray(np.asarray(kvb, f32)[:D].reshape(ET, 128).T)
    vb_row = np.ascontiguousarray(np.asarray(kvb, f32)[D:].reshape(1, D))
    rw_pack = np.ascontiguousarray(np.asarray(rw, f32).T).reshape(ET, 128,
                                                                  NEXP)
    rb_row = np.ascontiguousarray(np.asarray(rb, f32).reshape(1, NEXP))
    eb_a = np.ascontiguousarray(np.asarray(eb, f32))

    half = 64
    inv = 1.0 / (10000.0 ** (np.arange(half, dtype=np.float64) / half))
    pos = np.arange(S_TOT, dtype=np.float64)
    fr = pos[:, None] * inv[None, :]
    emb = np.concatenate([fr, fr], axis=1)
    cosT = np.ascontiguousarray(np.cos(emb).T.astype(f32))
    sinT = np.ascontiguousarray(np.sin(emb).T.astype(f32))
    sc = f32(1.0 / np.sqrt(128.0))
    cosq = (cosT * sc).astype(f32)
    sinq = (sinT * sc).astype(f32)

    idb = np.eye(128, dtype=bf16)
    rotm = np.zeros((128, 128), dtype=f32)
    for m in range(64):
        rotm[m + 64, m] = -1.0
        rotm[m, m + 64] = 1.0
    rotm = rotm.astype(bf16)
    onesb = np.ones((128, 128), dtype=bf16)
    onesf = np.ones((1, 1), dtype=f32)

    maps = []
    for c in range(NCORE):
        s0, s1 = c * S_CORE, (c + 1) * S_CORE
        maps.append(dict(
            x=np.ascontiguousarray(xs[s0:s1]),
            wq=wq_pack, wk=wk_pack, wv=wv_mov, wo=wo_mov,
            qb=qb_t, kb=kb_t, vb=vb_row,
            rw=rw_pack, rb=rb_row, eb=eb_a,
            cosq=np.ascontiguousarray(cosq[:, s0:s1]),
            sinq=np.ascontiguousarray(sinq[:, s0:s1]),
            cosk=np.ascontiguousarray(cosT[:, s0:s1]),
            sink=np.ascontiguousarray(sinT[:, s0:s1]),
            idb=idb, rotm=rotm, onesb=onesb, onesf=onesf,
        ))
    return maps


def kernel(L0_attn_x, L0_attn_q_w, L0_attn_q_b, L0_attn_kv_w, L0_attn_kv_b,
           L0_attn_o_w, L0_ffn_router_weight, L0_ffn_router_bias,
           L0_ffn_expert_bias):
    S_CORE = 2048
    maps = _host_prep(L0_attn_x, L0_attn_q_w, L0_attn_q_b, L0_attn_kv_w,
                      L0_attn_kv_b, L0_attn_o_w, L0_ffn_router_weight,
                      L0_ffn_router_bias, L0_ffn_expert_bias, S_CORE)
    nc = build(S_CORE=S_CORE, G_ST=4, T_SUP=4)
    res = run_bass_kernel_spmd(nc, maps, core_ids=list(range(NCORE)))
    y = np.concatenate([r["y"] for r in res.results], axis=0)
    return y.reshape(1, NCORE * S_CORE, D).astype(np.float32)


# version 06
# speedup vs baseline: 1.0264x; 1.0264x over previous
import numpy as np
import ml_dtypes
from contextlib import ExitStack

import concourse.bass as bass
import concourse.mybir as mybir
from concourse.bass_utils import run_bass_kernel_spmd

F32 = mybir.dt.float32
BF16 = mybir.dt.bfloat16
AF = mybir.ActivationFunctionType
ALU = mybir.AluOpType
AX = mybir.AxisListType

NCORE = 8
D = 2048
ET = D // 128
NEXP = 64
TOPK = 8
EPS = 1e-5


class Buf:
    __slots__ = ("w", "rs")

    def __init__(self):
        self.w = None
        self.rs = {}


class Dep:

    ENGS = ("t", "v", "s", "g", "q")

    def __init__(self, nc, ctx):
        keys = ["t", "v", "s", "cc", "gb", "qt",
                "qx0", "qx1", "gw0", "gw1", "qk0", "qk1", "qv0", "qv1", "qv2",
                "yw0", "yw1", "sp0", "sp1", "kp0", "kp1"]
        self.sem = {k: ctx.enter_context(nc.semaphore("sem_" + k)) for k in keys}
        self.cnt = {k: 0 for k in keys}
        self.seen = {e: {k: 0 for k in keys} for e in self.ENGS}

    def wait(self, eng, ename, tok):
        if tok is None:
            return
        key, v = tok
        if self.seen[ename][key] < v:
            eng.wait_ge(self.sem[key], v)
            self.seen[ename][key] = v

    def rd(self, eng, ename, *bufs):
        for b in bufs:
            self.wait(eng, ename, b.w)

    def wr(self, eng, ename, *bufs):
        for b in bufs:
            self.wait(eng, ename, b.w)
            for t in b.rs.values():
                self.wait(eng, ename, t)

    def inc(self, instr, key, n=1):
        instr.then_inc(self.sem[key], n)
        self.cnt[key] += n
        return (key, self.cnt[key])

    @staticmethod
    def did_w(tok, *bufs):
        for b in bufs:
            b.w = tok
            b.rs = {}

    @staticmethod
    def did_r(tok, *bufs):
        for b in bufs:
            b.rs[tok[0]] = tok

    def guard_merge(self, *bufs):
        if not hasattr(self, "guard"):
            self.guard = {}
        for b in bufs:
            toks = list(b.rs.values()) + ([b.w] if b.w else [])
            for k, v in toks:
                if k not in self.guard or self.guard[k][1] < v:
                    self.guard[k] = (k, v)

    def newbuf(self):
        b = Buf()
        if hasattr(self, "guard"):
            b.rs = dict(self.guard)
        return b


def build(S_CORE=2048, G_ST=4, T_SUP=4):
    S_TOT = NCORE * S_CORE
    NST = S_CORE // 128
    NG = NST // G_ST
    SW = G_ST * 128
    assert SW <= 512
    TP = T_SUP * 128
    NSUP = S_TOT // TP
    assert S_CORE % TP == 0
    SUP_PER_CORE = S_CORE // TP
    REG = ET * SW
    assert REG == G_ST * D
    NCH = D // 512

    nc = bass.Bass("TRN2", target_bir_lowering=False)

    x_h = nc.dram_tensor("x", [S_CORE, D], F32, kind="ExternalInput")
    wq_h = nc.dram_tensor("wq", [ET, 128, ET, 128], F32, kind="ExternalInput")
    wk_h = nc.dram_tensor("wk", [ET, 128, ET, 128], F32, kind="ExternalInput")
    wv_h = nc.dram_tensor("wv", [ET, 128, D], F32, kind="ExternalInput")
    wo_h = nc.dram_tensor("wo", [ET, 128, D], F32, kind="ExternalInput")
    qb_h = nc.dram_tensor("qb", [128, ET], F32, kind="ExternalInput")
    kb_h = nc.dram_tensor("kb", [128, ET], F32, kind="ExternalInput")
    vb_h = nc.dram_tensor("vb", [1, D], F32, kind="ExternalInput")
    rw_h = nc.dram_tensor("rw", [ET, 128, NEXP], F32, kind="ExternalInput")
    rb_h = nc.dram_tensor("rb", [1, NEXP], F32, kind="ExternalInput")
    eb_h = nc.dram_tensor("eb", [NEXP, D], F32, kind="ExternalInput")
    cosq_h = nc.dram_tensor("cosq", [128, S_CORE], F32, kind="ExternalInput")
    sinq_h = nc.dram_tensor("sinq", [128, S_CORE], F32, kind="ExternalInput")
    cosk_h = nc.dram_tensor("cosk", [128, S_CORE], F32, kind="ExternalInput")
    sink_h = nc.dram_tensor("sink", [128, S_CORE], F32, kind="ExternalInput")
    idb_h = nc.dram_tensor("idb", [128, 128], BF16, kind="ExternalInput")
    rotm_h = nc.dram_tensor("rotm", [128, 128], BF16, kind="ExternalInput")
    onesb_h = nc.dram_tensor("onesb", [128, 128], BF16, kind="ExternalInput")
    onesf_h = nc.dram_tensor("onesf", [1, 1], F32, kind="ExternalInput")
    y_h = nc.dram_tensor("y", [S_CORE, D], F32, kind="ExternalOutput")

    KVN = D * S_CORE
    kv_in = nc.dram_tensor("kv_in", [2 * KVN], BF16, kind="Internal")
    k_out = nc.dram_tensor("k_out", [NCORE * KVN], BF16, kind="Internal",
                           addr_space="Shared")
    v_out = nc.dram_tensor("v_out", [NCORE * KVN], BF16, kind="Internal",
                           addr_space="Shared")

    with ExitStack() as ctx:
        en = ctx.enter_context
        dep = Dep(nc, ctx)
        block = en(nc.Block())

        def T(f):
            block.tensor(f)

        def V(f):
            block.vector(f)

        def S(f):
            block.scalar(f)

        def Gp(f):
            block.gpsimd(f)

        def Q(f):
            block.sync(f)

        idb = en(nc.sbuf_tensor("idb_s", [128, 128], BF16))
        rotm = en(nc.sbuf_tensor("rotm_s", [128, 128], BF16))
        onesb = en(nc.sbuf_tensor("onesb_s", [128, 128], BF16))
        onesf = en(nc.sbuf_tensor("onesf_s", [1, 1], F32))
        qb_s = en(nc.sbuf_tensor("qb_s", [128, ET], F32))
        kb_s = en(nc.sbuf_tensor("kb_s", [128, ET], F32))
        xt = [en(nc.sbuf_tensor(f"xt{i}", [128, D], F32)) for i in range(2)]
        b_wo_c0 = Buf()
        ln = en(nc.sbuf_tensor("ln", [128, 32], F32))

        b_xt = [Buf(), Buf()]
        b_ln = Buf()
        b_qTg = [Buf() for _ in range(NG)]
        b_kvin = Buf()
        b_consts = Buf()
        b_psA = Buf()
        b_psAh = [Buf(), Buf()]
        b_psB = [Buf(), Buf()]
        b_psRS = Buf()
        b_psSp = Buf()

        psA = en(nc.psum_tensor("psA", [128, 2048], F32))
        psB = en(nc.psum_tensor("psB", [128, 2048], F32))

        def consts(g):
            pairs = [(idb, idb_h), (rotm, rotm_h), (onesb, onesb_h),
                     (onesf, onesf_h), (qb_s, qb_h), (kb_s, kb_h)]
            t = None
            for dst, src in pairs:
                g.wait_ge(dep.sem["gb"], dep.cnt["gb"])
                t = dep.inc(g.dma_start(out=dst[:, :], in_=src[:, :]),
                            "gb", 16)
            dep.did_w(t, b_consts)

        Gp(consts)

        def preload_tbl(sc):
            from concourse.hw_specs import get_activation_tables
            for idx, nm in enumerate(get_activation_tables(nc.m.arch)):
                if nm == "natural_log_exp_and_others":
                    sc.add_instruction(mybir.InstLoadActFuncSet(
                        name=nc.get_next_instruction_name(),
                        act_func_set_id=idx, ins=[], outs=[]))
                    return
            raise ValueError("no natural_log_exp_and_others table set")

        S(preload_tbl)

        c12 = ctx.enter_context(ExitStack())
        xnT = c12.enter_context(nc.sbuf_tensor("xnT", [128, ET, S_CORE],
                                               BF16))
        b_xnT = Buf()
        with ExitStack() as c1:
            wv_s = c1.enter_context(nc.sbuf_tensor("wv_s", [128, ET, D], BF16))
            vb_s = c1.enter_context(nc.sbuf_tensor("vb_s", [1, D], BF16))
            xnt = [c1.enter_context(nc.sbuf_tensor(f"xnt{i}", [128, D], BF16))
                   for i in range(2)]
            scr = c1.enter_context(nc.sbuf_tensor("scr", [128, D], F32))
            b_xnt = [Buf(), Buf()]
            b_scr = Buf()
            vst = [c1.enter_context(nc.sbuf_tensor(f"vst{i}", [128, D], BF16))
                   for i in range(2)]
            b_wv = [Buf() for _ in range(4)]
            b_vst = [Buf(), Buf()]

            def wv_load(g):
                g.wait_ge(dep.sem["gb"], dep.cnt["gb"])
                dep.inc(g.dma_start(out=vb_s[:, :], in_=vb_h[:, :]), "gb", 16)
                for c in range(4):
                    g.wait_ge(dep.sem["gb"], dep.cnt["gb"])
                    t = dep.inc(g.dma_start(
                        out=wv_s[:, c * 4:(c + 1) * 4, :],
                        in_=wv_h[c * 4:(c + 1) * 4, :, :]
                        .rearrange("a p n -> p a n")), "gb", 16)
                    dep.did_w(t, b_wv[c])

            Gp(wv_load)

            def xload(q, st, sl):
                dep.wr(q, "q", b_xt[sl])
                t = dep.inc(q.dma_start(out=xt[sl][:, :],
                                        in_=x_h[st * 128:(st + 1) * 128, :]),
                            "qx%d" % sl, 16)
                dep.did_w(t, b_xt[sl])

            Q(lambda q: xload(q, 0, 0))

            for st in range(NST):
                sl = st % 2
                if st + 1 < NST:
                    Q(lambda q, st=st, sl=sl: xload(q, st + 1, 1 - sl))

                def ln1(v, sl=sl):
                    dep.rd(v, "v", b_xt[sl])
                    dep.wr(v, "v", b_ln, b_scr)
                    v.reduce_sum(ln[:, 0:1], xt[sl][:, :], axis=AX.X)
                    t = dep.inc(v.scalar_tensor_tensor(
                        scr[:, :], xt[sl][:, :], 1.0, xt[sl][:, :],
                        op0=ALU.mult, op1=ALU.mult, accum_out=ln[:, 1:2]), "v")
                    dep.did_w(t, b_scr)
                    dep.did_r(t, b_xt[sl])
                    v.wait_ge(dep.sem["v"], dep.cnt["v"])
                    v.tensor_scalar(ln[:, 2:3], ln[:, 0:1], 1.0 / D, None,
                                    op0=ALU.mult)
                    t = dep.inc(v.tensor_scalar(ln[:, 3:4], ln[:, 1:2],
                                                -1.0 / D, -EPS,
                                                op0=ALU.mult, op1=ALU.add), "v")
                    v.wait_ge(dep.sem["v"], dep.cnt["v"])
                    t = dep.inc(v.scalar_tensor_tensor(
                        ln[:, 4:5], ln[:, 2:3], ln[:, 2:3], ln[:, 3:4],
                        op0=ALU.mult, op1=ALU.add), "v")
                    dep.did_w(t, b_ln)

                V(ln1)

                def ln2(s, sl=sl):
                    dep.rd(s, "s", b_ln)
                    t = dep.inc(s.activation(ln[:, 5:6], ln[:, 4:5],
                                             AF.Ln, scale=-1.0), "s")
                    s.wait_ge(dep.sem["s"], t[1])
                    t = dep.inc(s.activation(ln[:, 6:7], ln[:, 5:6], AF.Exp,
                                             scale=-0.5), "s")
                    dep.did_w(t, b_ln)

                S(ln2)

                def ln3(v, sl=sl):
                    dep.rd(v, "v", b_ln)
                    t = dep.inc(v.scalar_tensor_tensor(
                        ln[:, 7:8], ln[:, 2:3], -1.0, ln[:, 6:7],
                        op0=ALU.mult, op1=ALU.mult), "v")
                    dep.did_w(t, b_ln)

                V(ln3)

                def ln4(s, sl=sl):
                    dep.rd(s, "s", b_ln, b_xt[sl])
                    dep.wr(s, "s", b_xnt[sl])
                    t = dep.inc(s.activation(xnt[sl][:, :], xt[sl][:, :],
                                             AF.Identity, bias=ln[:, 7:8],
                                             scale=ln[:, 6:7]), "s")
                    dep.did_w(t, b_xnt[sl])
                    dep.did_r(t, b_xt[sl], b_ln)

                S(ln4)

                for b4 in range(4):
                    slot = b4 % 2

                    def tr(t_, st=st, sl=sl, b4=b4, slot=slot):
                        dep.rd(t_, "t", b_xnt[sl], b_consts)
                        dep.wr(t_, "t", b_psB[slot])
                        for b in range(4):
                            jt = b4 * 4 + b
                            li = t_.matmul(psB[:, slot * 512 + b * 128:
                                          slot * 512 + (b + 1) * 128],
                                      xnt[sl][:, jt * 128:(jt + 1) * 128],
                                      idb[:, :], start=True, stop=True)
                        tok = dep.inc(li, "t")
                        dep.did_w(tok, b_psB[slot])
                        dep.did_r(tok, b_xnt[sl])

                    T(tr)

                    def ev(s, st=st, b4=b4, slot=slot):
                        dep.rd(s, "s", b_psB[slot])
                        dep.wr(s, "s", b_xnT)
                        tok = dep.inc(s.copy(
                            xnT[:, b4 * 4:(b4 + 1) * 4,
                                st * 128:(st + 1) * 128],
                            psB[:, slot * 512:(slot + 1) * 512]
                            .rearrange("p (a n) -> p a n", n=128)), "s")
                        b_xnT.w = tok
                        dep.did_r(tok, b_psB[slot])

                    S(ev)

                def vproj(t_, st=st):
                    dep.rd(t_, "t", b_xnT, b_consts)
                    dep.wr(t_, "t", b_psA)
                    for jt in range(ET):
                        dep.rd(t_, "t", b_wv[jt // 4])
                        for n in range(NCH):
                            li = t_.matmul(psA[:, n * 512:(n + 1) * 512],
                                      xnT[:, jt, st * 128:(st + 1) * 128],
                                      wv_s[:, jt, n * 512:(n + 1) * 512],
                                      start=(jt == 0), stop=False)
                    for n in range(NCH):
                        li = t_.matmul(psA[:, n * 512:(n + 1) * 512],
                                  onesb[0:1, :], vb_s[0:1, n * 512:(n + 1) * 512],
                                  start=False, stop=True)
                    tok = dep.inc(li, "t")
                    dep.did_w(tok, b_psA)
                    dep.did_r(tok, b_xnT)
                    for c in range(4):
                        dep.did_r(tok, b_wv[c])

                T(vproj)

                def vev(s, st=st, sl=sl):
                    dep.rd(s, "s", b_psA)
                    dep.wr(s, "s", b_vst[sl])
                    tok = dep.inc(s.copy(vst[sl][:, :], psA[:, :]), "s")
                    dep.did_w(tok, b_vst[sl])
                    dep.did_r(tok, b_psA)

                S(vev)

                def vpub(s, st=st, sl=sl):
                    dep.rd(s, "s", b_vst[sl])
                    off = st * 128 * D
                    tok = dep.inc(s.dma_start(
                        out=kv_in[off:off + 128 * D]
                        .rearrange("(p n) -> p n", p=128),
                        in_=vst[sl][:, :]), "sp%d" % sl, 16)
                    dep.did_r(tok, b_vst[sl], b_kvin)

                S(vpub)

            dep.guard_merge(*b_wv, b_vst[0], b_vst[1], b_psA, b_xnt[0],
                            b_xnt[1])

        b_kvoutv = Buf()
        b_kvout = Buf()

        def ag_v(g):
            for t in list(b_kvin.rs.values()):
                dep.wait(g, "g", t)
            tok = dep.inc(g.collective_compute(
                "AllGather", ALU.bypass,
                replica_groups=[list(range(NCORE))],
                ins=[kv_in[0:KVN].opt()],
                outs=[v_out[:].opt()]), "cc")
            dep.did_w(tok, b_kvoutv)

        Gp(ag_v)

        qT = ctx.enter_context(nc.sbuf_tensor("qT", [128, NG * REG], BF16,
                                              side="right"))

        with ExitStack() as c2:
            csq = c2.enter_context(nc.sbuf_tensor("csq", [128, S_CORE], F32))
            snq = c2.enter_context(nc.sbuf_tensor("snq", [128, S_CORE], F32))
            wcol = [c2.enter_context(
                nc.sbuf_tensor(f"wcol{i}", [128, ET, 128], BF16))
                for i in range(2)]
            tq = [c2.enter_context(nc.sbuf_tensor(f"tq{i}", [128, S_CORE],
                                                  BF16)) for i in range(2)]
            m1 = c2.enter_context(nc.sbuf_tensor("m1", [128, S_CORE], BF16))
            m2 = c2.enter_context(nc.sbuf_tensor("m2", [128, S_CORE], BF16))
            kst = [c2.enter_context(nc.sbuf_tensor(f"kst{i}", [128, S_CORE],
                                                   BF16)) for i in range(2)]
            b_tab = dep.newbuf()
            b_wcol = [dep.newbuf(), dep.newbuf()]
            b_tq = [dep.newbuf(), dep.newbuf()]
            b_m = dep.newbuf()
            b_kst = [dep.newbuf(), dep.newbuf()]

            def tabs(q, ch, sh):
                dep.wr(q, "q", b_tab)
                dep.inc(q.dma_start(out=csq[:, :], in_=ch[:, :]), "qt", 16)
                t = dep.inc(q.dma_start(out=snq[:, :], in_=sh[:, :]),
                            "qt", 16)
                dep.did_w(t, b_tab)

            Q(lambda q: tabs(q, cosk_h, sink_h))

            def wload(g, it, sl, wh):
                dep.wr(g, "g", b_wcol[sl])
                t = dep.inc(g.dma_start(
                    out=wcol[sl][:, :, :],
                    in_=wh[it, :, :, :]), "gw%d" % sl, 16)
                dep.did_w(t, b_wcol[sl])

            Gp(lambda g: wload(g, 0, 0, wk_h))

            W1 = min(512, S_CORE)
            NSC = S_CORE // W1
            for it in range(2 * ET):
                isq = it >= ET
                et = it % ET
                sl = it % 2
                if it == ET:
                    Q(lambda q: tabs(q, cosq_h, sinq_h))

                    def ag_k(g):
                        for t in list(b_kvin.rs.values()):
                            dep.wait(g, "g", t)
                        tok = dep.inc(g.collective_compute(
                            "AllGather", ALU.bypass,
                            replica_groups=[list(range(NCORE))],
                            ins=[kv_in[KVN:2 * KVN].opt()],
                            outs=[k_out[:].opt()]), "cc")
                        dep.did_w(tok, b_kvout)

                    Gp(ag_k)
                if it + 1 < 2 * ET:
                    nxt = it + 1
                    Gp(lambda g, nxt=nxt, sl=sl: wload(
                        g, nxt % ET, 1 - sl, wk_h if nxt < ET else wq_h))

                pt = psA if it % 2 == 0 else psB
                pbufs = ([b_psA] if it % 2 == 0 else
                         [b_psB[0], b_psB[1], b_psRS, b_psSp])

                def proj(t_, it=it, sl=sl, pt=pt, pbufs=pbufs):
                    dep.rd(t_, "t", b_wcol[sl], b_xnT)
                    for _b in pbufs:
                        dep.wr(t_, "t", _b)
                    for jt in range(ET):
                        for n in range(NSC):
                            li = t_.matmul(pt[:, n * W1:(n + 1) * W1],
                                      wcol[sl][:, jt, :],
                                      xnT[:, jt, n * W1:(n + 1) * W1],
                                      start=(jt == 0), stop=(jt == ET - 1))
                    tok = dep.inc(li, "t")
                    dep.did_w(tok, *pbufs)
                    dep.did_r(tok, b_wcol[sl], b_xnT)

                T(proj)

                def bias(s, it=it, isq=isq, et=et, pt=pt, pbufs=pbufs,
                         sl=sl):
                    for _b in pbufs:
                        dep.rd(s, "s", _b)
                    dep.wr(s, "s", b_tq[sl])
                    bb = qb_s if isq else kb_s
                    tok = dep.inc(s.activation(tq[sl][:, :], pt[:, 0:S_CORE],
                                               AF.Identity,
                                               bias=bb[:, et:et + 1]), "s")
                    dep.did_w(tok, b_tq[sl])
                    for _b in pbufs:
                        dep.did_r(tok, _b)

                S(bias)

                def rot(t_, it=it, pt=pt, pbufs=pbufs, sl=sl):
                    dep.rd(t_, "t", b_tq[sl], b_consts)
                    for _b in pbufs:
                        dep.wr(t_, "t", _b)
                    for n in range(NSC):
                        li = t_.matmul(pt[:, n * W1:(n + 1) * W1], rotm[:, :],
                                  tq[sl][:, n * W1:(n + 1) * W1],
                                  start=True, stop=True)
                    tok = dep.inc(li, "t")
                    dep.did_w(tok, *pbufs)
                    dep.did_r(tok, b_tq[sl])

                T(rot)

                def rope(v, it=it, isq=isq, et=et, sl=sl, pt=pt, pbufs=pbufs):
                    cs, sn = csq, snq
                    dep.rd(v, "v", b_tq[sl], b_tab)
                    for _b in pbufs:
                        dep.rd(v, "v", _b)
                    dep.wr(v, "v", b_m)
                    v.tensor_mul(m1[:, :], tq[sl][:, :], cs[:, :])
                    tok = dep.inc(v.tensor_mul(m2[:, :], pt[:, 0:S_CORE],
                                               sn[:, :]), "v")
                    dep.did_w(tok, b_m)
                    dep.did_r(tok, b_tq[sl], b_tab)
                    for _b in pbufs:
                        dep.did_r(tok, _b)
                    v.wait_ge(dep.sem["v"], tok[1])
                    if isq:
                        for g in range(NG):
                            dep.wr(v, "v", b_qTg[g])
                        dst = qT[:, :].rearrange("p (g r) -> p g r", g=NG)[
                            :, :, et * SW:(et + 1) * SW]
                        src1 = m1[:, :].rearrange("p (g r) -> p g r", g=NG)
                        src2 = m2[:, :].rearrange("p (g r) -> p g r", g=NG)
                        tok = dep.inc(v.tensor_add(dst, src1, src2), "v")
                        for g in range(NG):
                            dep.did_w(tok, b_qTg[g])
                    else:
                        dep.wr(v, "v", b_kst[sl])
                        tok = dep.inc(v.tensor_add(kst[sl][:, :], m1[:, :],
                                                   m2[:, :]), "v")
                        dep.did_w(tok, b_kst[sl])
                    dep.did_r(tok, b_m)

                V(rope)

                if not isq:
                    def kpub(s, et=et, sl=sl):
                        dep.rd(s, "s", b_kst[sl])
                        off = KVN + et * 128 * S_CORE
                        tok = dep.inc(s.dma_start(
                            out=kv_in[off:off + 128 * S_CORE]
                            .rearrange("(p n) -> p n", p=128),
                            in_=kst[sl][:, :]), "kp%d" % sl, 16)
                        dep.did_r(tok, b_kst[sl], b_kvin)

                    S(kpub)

            dep.guard_merge(b_tab, b_wcol[0], b_wcol[1], b_tq[0], b_tq[1],
                            b_m, b_kst[0], b_kst[1])
        dep.guard_merge(b_xnT)
        c12.close()


        wo_c0 = ctx.enter_context(nc.sbuf_tensor("wo_c0", [128, ET, 512],
                                                 BF16, side="right"))

        with ExitStack() as c3:
            kbuf = [c3.enter_context(nc.sbuf_tensor(f"kbuf{i}", [128, ET, TP],
                                                    BF16)) for i in range(2)]
            vbuf = [c3.enter_context(nc.sbuf_tensor(f"vbuf{i}",
                                                    [128, T_SUP, D], BF16))
                    for i in range(2)]
            expT = [c3.enter_context(nc.sbuf_tensor(f"expT{i}",
                                                    [128, T_SUP, SW], BF16))
                    for i in range(2)]
            o_acc = c3.enter_context(nc.sbuf_tensor("o_acc",
                                                    [128, G_ST, D], F32))
            rrow = c3.enter_context(nc.sbuf_tensor("rrow", [1, SW], F32))
            rrec = c3.enter_context(nc.sbuf_tensor("rrec", [1, SW], F32))
            rct = c3.enter_context(nc.sbuf_tensor("rct", [128, G_ST], F32))
            def _tmax(a, b):
                if a is None:
                    return b
                if b is None:
                    return a
                return a if a[1] >= b[1] else b

            b_psAh[0].w = _tmax(b_psAh[0].w, b_psA.w)
            b_psAh[1].w = _tmax(b_psAh[1].w, b_psA.w)
            for _k, _t in b_psA.rs.items():
                b_psAh[0].rs[_k] = _t
                b_psAh[1].rs[_k] = _t
            b_kb = [dep.newbuf(), dep.newbuf()]
            b_vb = [dep.newbuf(), dep.newbuf()]
            b_ex = [dep.newbuf(), dep.newbuf()]
            b_oac = [dep.newbuf() for _ in range(G_ST)]
            b_r = dep.newbuf()
            b_rct = dep.newbuf()

            def kload(q, su_idx):
                su = su_idx % NSUP
                ks = su_idx % 2
                c = (su * TP) // S_CORE
                r0 = (su * TP) % S_CORE
                base = c * KVN
                dep.rd(q, "q", b_kvout)
                dep.wr(q, "q", b_kb[ks])
                tok = dep.inc(q.dma_start(
                    out=kbuf[ks][:, :, :],
                    in_=k_out[base:base + KVN]
                    .rearrange("(a p s) -> p a s", p=128, s=S_CORE)
                    [:, :, r0:r0 + TP]), "qk%d" % ks, 16)
                dep.did_w(tok, b_kb[ks])

            def vload(q, su_idx):
                su = su_idx % NSUP
                vs = su_idx % 2
                c = (su * TP) // S_CORE
                r0 = (su * TP) % S_CORE
                base = c * KVN
                dep.rd(q, "q", b_kvoutv)
                dep.wr(q, "q", b_vb[vs])
                vb0 = base + r0 * D
                tok = dep.inc(q.dma_start(
                    out=vbuf[vs][:, :, :],
                    in_=v_out[vb0:vb0 + TP * D]
                    .rearrange("(a p e) -> p a e", p=128, e=D)), "qv%d" % vs,
                    16)
                dep.did_w(tok, b_vb[vs])

            Q(lambda q: kload(q, 0))
            Q(lambda q: vload(q, 0))

            def wo_c0_load(g):
                g.wait_ge(dep.sem["gb"], dep.cnt["gb"])
                tok = dep.inc(g.dma_start(
                    out=wo_c0[:, :, :],
                    in_=wo_h[:, :, 0:512].rearrange("a p n -> p a n")),
                    "gb", 16)
                dep.did_w(tok, b_wo_c0)

            Gp(wo_c0_load)

            def scores_block(g, ssu, su_idx):
                ks = su_idx % 2
                ex = su_idx % 2
                for tt in range(T_SUP):
                    pb = tt % 2

                    def sc(t_, g=g, tt=tt, ks=ks, pb=pb):
                        dep.rd(t_, "t", b_kb[ks], b_qTg[g])
                        dep.wr(t_, "t", b_psB[pb])
                        for et in range(ET):
                            li = t_.matmul(psB[:, pb * 512:pb * 512 + SW],
                                           kbuf[ks][:, et,
                                                    tt * 128:(tt + 1) * 128],
                                           qT[:, g * REG + et * SW:
                                              g * REG + (et + 1) * SW],
                                           start=(et == 0),
                                           stop=(et == ET - 1))
                        tok = dep.inc(li, "t")
                        dep.did_w(tok, b_psB[pb])
                        dep.did_r(tok, b_kb[ks], b_qTg[g])

                    T(sc)

                    def ex_(s, tt=tt, ex=ex, pb=pb):
                        dep.rd(s, "s", b_psB[pb])
                        dep.wr(s, "s", b_ex[ex])
                        tok = dep.inc(s.activation(
                            expT[ex][:, tt, :],
                            psB[:, pb * 512:pb * 512 + SW], AF.Exp), "s")
                        b_ex[ex].w = tok
                        dep.did_r(tok, b_psB[pb])

                    S(ex_)

            def pv_block(g, ssu, su_idx):
                ex = su_idx % 2
                vs = su_idx % 2

                def rsum(t_, g=g, ssu=ssu, ex=ex):
                    dep.rd(t_, "t", b_ex[ex], b_consts)
                    if ssu == 0:
                        dep.wr(t_, "t", b_psRS)
                    for tt in range(T_SUP):
                        li = t_.matmul(psB[0:1, 1024:1024 + SW],
                                       onesb[:, 0:1], expT[ex][:, tt, :],
                                       start=(ssu == 0 and tt == 0),
                                       stop=(ssu == NSUP - 1 and
                                             tt == T_SUP - 1))
                    tok = dep.inc(li, "t")
                    dep.did_w(tok, b_psRS)
                    dep.did_r(tok, b_ex[ex])

                T(rsum)

                for stl in range(G_ST):
                    def pv(t_, ssu=ssu, vs=vs, ex=ex, stl=stl):
                        dep.rd(t_, "t", b_ex[ex], b_vb[vs])
                        for h in range(2):
                            dep.wr(t_, "t", b_psAh[h])
                            for tt in range(T_SUP):
                                for n in (0, 1):
                                    nn = h * 2 + n
                                    li = t_.matmul(
                                        psA[:, nn * 512:(nn + 1) * 512],
                                        expT[ex][:, tt,
                                                 stl * 128:(stl + 1) * 128],
                                        vbuf[vs][:, tt,
                                                 nn * 512:(nn + 1) * 512],
                                        start=(tt == 0),
                                        stop=(tt == T_SUP - 1))
                            tok = dep.inc(li, "t")
                            dep.did_w(tok, b_psAh[h])
                            dep.did_r(tok, b_ex[ex], b_vb[vs])

                    T(pv)

                    def fl(v, ssu=ssu, stl=stl):
                        for h in range(2):
                            dep.rd(v, "v", b_psAh[h])
                            dep.wr(v, "v", b_oac[stl])
                            dst = o_acc[:, stl, h * 1024:(h + 1) * 1024]
                            if ssu == 0:
                                ins = v.tensor_copy(
                                    dst, psA[:, h * 1024:(h + 1) * 1024])
                            else:
                                ins = v.tensor_add(
                                    dst, dst,
                                    psA[:, h * 1024:(h + 1) * 1024])
                            tok = dep.inc(ins, "v")
                            dep.did_w(tok, b_oac[stl])
                            dep.did_r(tok, b_psAh[h])

                    V(fl)

            def sweep_end(g):
                def rfin_s(s, g=g):
                    dep.rd(s, "s", b_psRS)
                    dep.wr(s, "s", b_r)
                    tok = dep.inc(s.copy(rrow[0:1, :],
                                         psB[0:1, 1024:1024 + SW]), "s")
                    dep.did_w(tok, b_r)
                    dep.did_r(tok, b_psRS)

                S(rfin_s)

                def rfin_v(v, g=g):
                    dep.rd(v, "v", b_r)
                    tok = dep.inc(v.reciprocal(rrec[0:1, :], rrow[0:1, :]),
                                  "v")
                    dep.did_w(tok, b_r)

                V(rfin_v)

                def rtr(t_, g=g):
                    dep.rd(t_, "t", b_r, b_consts)
                    dep.wr(t_, "t", b_psSp)
                    for stl in range(G_ST):
                        li = t_.matmul(psB[:, 1536 + stl:1536 + stl + 1],
                                       rrec[0:1, stl * 128:(stl + 1) * 128],
                                       onesf[0:1, 0:1], start=True, stop=True)
                    tok = dep.inc(li, "t")
                    dep.did_w(tok, b_psSp)
                    dep.did_r(tok, b_r)

                T(rtr)

                def rev(s, g=g):
                    dep.rd(s, "s", b_psSp)
                    dep.wr(s, "s", b_rct)
                    tok = dep.inc(s.copy(rct[:, :],
                                         psB[:, 1536:1536 + G_ST]), "s")
                    dep.did_w(tok, b_rct)
                    dep.did_r(tok, b_psSp)

                S(rev)

                for stl in range(G_ST):
                    def obf(v, g=g, stl=stl):
                        dep.rd(v, "v", b_oac[stl], b_rct)
                        dep.wr(v, "v", b_qTg[g])
                        dst = qT[:, g * REG + stl * D:g * REG + (stl + 1) * D]
                        tok = dep.inc(v.tensor_scalar(
                            dst, o_acc[:, stl, :], rct[:, stl:stl + 1], None,
                            op0=ALU.mult), "v")
                        b_qTg[g].w = tok
                        dep.did_r(tok, b_oac[stl], b_rct)

                    V(obf)

            NSU_ALL = NG * NSUP
            for su_idx in range(NSU_ALL + 1):
                if su_idx < NSU_ALL:
                    g, ssu = divmod(su_idx, NSUP)
                    if su_idx + 1 < NSU_ALL:
                        Q(lambda q, si=su_idx + 1: kload(q, si))
                    scores_block(g, ssu, su_idx)
                if su_idx >= 1:
                    pg, pssu = divmod(su_idx - 1, NSUP)
                    pv_block(pg, pssu, su_idx - 1)
                    if pssu == NSUP - 1:
                        sweep_end(pg)
                if 1 <= su_idx + 1 < NSU_ALL:
                    Q(lambda q, si=su_idx + 1: vload(q, si))

            dep.guard_merge(b_kb[0], b_kb[1], b_vb[0], b_vb[1], b_ex[0],
                            b_ex[1], *b_oac, b_r, b_rct)

        def _tmax2(a, b):
            if a is None:
                return b
            if b is None:
                return a
            return a if a[1] >= b[1] else b

        b_psA.w = _tmax2(_tmax2(b_psA.w, b_psAh[0].w), b_psAh[1].w)
        for _h in (0, 1):
            for _k, _t in b_psAh[_h].rs.items():
                if _k not in b_psA.rs or b_psA.rs[_k][1] < _t[1]:
                    b_psA.rs[_k] = _t

        with ExitStack() as c4:
            wo_s = c4.enter_context(nc.sbuf_tensor("wo_s", [128, ET, 3 * 512],
                                                   BF16))
            rw_s = c4.enter_context(nc.sbuf_tensor("rw_s", [128, ET, NEXP],
                                                   BF16))
            rb_s = c4.enter_context(nc.sbuf_tensor("rb_s", [1, NEXP], BF16))
            eb_s = c4.enter_context(nc.sbuf_tensor("eb_s", [NEXP, D], BF16))
            scr = c4.enter_context(nc.sbuf_tensor("scr2", [128, D], F32))
            b_scr = dep.newbuf()
            oT = [c4.enter_context(nc.sbuf_tensor(f"oT{i}", [128, ET, 128],
                                                  BF16)) for i in range(2)]
            x2T = c4.enter_context(nc.sbuf_tensor("x2T", [128, ET, 128],
                                                  BF16))
            x1 = [c4.enter_context(nc.sbuf_tensor(f"x1_{i}", [128, D], F32))
                  for i in range(2)]
            xn2 = [c4.enter_context(nc.sbuf_tensor(f"xn2_{i}", [128, D],
                                                   BF16)) for i in range(2)]
            x2 = c4.enter_context(nc.sbuf_tensor("x2_", [128, D], F32))
            pp = c4.enter_context(nc.sbuf_tensor("pp", [128, NEXP], F32))
            wk64 = c4.enter_context(nc.sbuf_tensor("wk64", [128, NEXP], F32))
            top8 = c4.enter_context(nc.sbuf_tensor("top8", [128, TOPK], F32))
            msk = c4.enter_context(nc.sbuf_tensor("msk", [128, NEXP], F32))
            mpb = c4.enter_context(nc.sbuf_tensor("mpb", [128, NEXP], BF16))
            mpT = c4.enter_context(nc.sbuf_tensor("mpT", [64, 128], BF16))
            b_wo = [dep.newbuf() for _ in range(NCH)]
            b_rwc = dep.newbuf()
            b_oT = [dep.newbuf(), dep.newbuf()]
            b_x2T = dep.newbuf()
            b_x1 = [dep.newbuf(), dep.newbuf()]
            b_xn2 = [dep.newbuf(), dep.newbuf()]
            b_x2 = dep.newbuf()
            b_p = dep.newbuf()
            b_mp = dep.newbuf()
            b_mpT = dep.newbuf()
            b_lnp = [b_ln, dep.newbuf()]

            def wo_load(g):
                dep.wr(g, "g", b_rwc)
                g.wait_ge(dep.sem["gb"], dep.cnt["gb"])
                dep.inc(g.dma_start(
                    out=rw_s[:, :, :],
                    in_=rw_h[:, :, :].rearrange("a p n -> p a n")), "gb", 16)
                g.wait_ge(dep.sem["gb"], dep.cnt["gb"])
                dep.inc(g.dma_start(out=rb_s[:, :], in_=rb_h[:, :]), "gb", 16)
                g.wait_ge(dep.sem["gb"], dep.cnt["gb"])
                tok = dep.inc(g.dma_start(out=eb_s[:, :], in_=eb_h[:, :]),
                              "gb", 16)
                dep.did_w(tok, b_rwc)
                for n in range(1, NCH):
                    dep.wr(g, "g", b_wo[n])
                    g.wait_ge(dep.sem["gb"], dep.cnt["gb"])
                    tok = dep.inc(g.dma_start(
                        out=wo_s[:, :, (n - 1) * 512:n * 512],
                        in_=wo_h[:, :, n * 512:(n + 1) * 512]
                        .rearrange("a p n -> p a n")), "gb", 16)
                    dep.did_w(tok, b_wo[n])

            Gp(wo_load)
            Q(lambda q: xload(q, 0, 0))

            def emit_otr(st):
                sl = st % 2
                g = st // G_ST
                stl = st % G_ST
                for b4 in range(4):
                    slot = b4 % 2

                    def otr(t_, g=g, stl=stl, b4=b4, slot=slot):
                        dep.rd(t_, "t", b_qTg[g], b_consts)
                        dep.wr(t_, "t", b_psB[slot])
                        ob = qT[:, g * REG + stl * D:g * REG + (stl + 1) * D]
                        for b in range(4):
                            eb_ = b4 * 4 + b
                            li = t_.matmul(psB[:, slot * 512 + b * 128:
                                               slot * 512 + (b + 1) * 128],
                                           ob[:, eb_ * 128:(eb_ + 1) * 128],
                                           idb[:, :], start=True, stop=True)
                        tok = dep.inc(li, "t")
                        dep.did_w(tok, b_psB[slot])
                        dep.did_r(tok, b_qTg[g])

                    T(otr)

                    def oev(s, st=st, b4=b4, slot=slot, sl=sl):
                        dep.rd(s, "s", b_psB[slot])
                        dep.wr(s, "s", b_oT[sl])
                        tok = dep.inc(s.copy(
                            oT[sl][:, b4 * 4:(b4 + 1) * 4, :],
                            psB[:, slot * 512:(slot + 1) * 512]
                            .rearrange("p (a n) -> p a n", n=128)), "s")
                        b_oT[sl].w = tok
                        dep.did_r(tok, b_psB[slot])

                    S(oev)

            def emit_yproj(st):
                sl = st % 2

                def yproj(t_, sl=sl):
                    dep.rd(t_, "t", b_oT[sl])
                    dep.wr(t_, "t", b_psA)
                    for n in range(NCH):
                        if n == 0:
                            dep.rd(t_, "t", b_wo_c0)
                        else:
                            dep.rd(t_, "t", b_wo[n])
                        for et in range(ET):
                            src = (wo_c0[:, et, :] if n == 0 else
                                   wo_s[:, et, (n - 1) * 512:n * 512])
                            li = t_.matmul(psA[:, n * 512:(n + 1) * 512],
                                           oT[sl][:, et, :], src,
                                           start=(et == 0),
                                           stop=(et == ET - 1))
                    tok = dep.inc(li, "t")
                    dep.did_w(tok, b_psA)
                    dep.did_r(tok, b_oT[sl], b_wo_c0)
                    for n in range(1, NCH):
                        dep.did_r(tok, b_wo[n])

                T(yproj)

            emit_otr(0)
            emit_yproj(0)

            for st in range(NST):
                sl = st % 2
                g = st // G_ST
                stl = st % G_ST
                ln_o = 16 * sl
                b_lnc = b_lnp[sl]
                if st + 1 < NST:
                    Q(lambda q, st=st: xload(q, st + 1, (st + 1) % 2))

                def resid(v, sl=sl, ln_o=ln_o, b_lnc=b_lnc):
                    dep.rd(v, "v", b_psA, b_xt[sl])
                    dep.wr(v, "v", b_x1[sl], b_lnc)
                    tok = dep.inc(v.scalar_tensor_tensor(
                        x1[sl][:, :], psA[:, :], 1.0, xt[sl][:, :],
                        op0=ALU.mult, op1=ALU.add,
                        accum_out=ln[:, ln_o + 0:ln_o + 1]), "v")
                    dep.did_w(tok, b_x1[sl])
                    b_lnc.w = tok
                    dep.did_r(tok, b_psA, b_xt[sl])

                V(resid)

                if st + 1 < NST:
                    emit_otr(st + 1)
                    emit_yproj(st + 1)

                def l1(v, sl=sl, ln_o=ln_o, b_lnc=b_lnc):
                    dep.rd(v, "v", b_x1[sl], b_lnc)
                    dep.wr(v, "v", b_scr)
                    t = dep.inc(v.scalar_tensor_tensor(
                        scr[:, :], x1[sl][:, :], 1.0, x1[sl][:, :],
                        op0=ALU.mult, op1=ALU.mult,
                        accum_out=ln[:, ln_o + 1:ln_o + 2]), "v")
                    dep.did_w(t, b_scr)
                    dep.did_r(t, b_x1[sl])
                    v.wait_ge(dep.sem["v"], dep.cnt["v"])
                    v.tensor_scalar(ln[:, ln_o + 2:ln_o + 3],
                                    ln[:, ln_o + 0:ln_o + 1], 1.0 / D, None,
                                    op0=ALU.mult)
                    dep.inc(v.tensor_scalar(ln[:, ln_o + 3:ln_o + 4],
                                            ln[:, ln_o + 1:ln_o + 2],
                                            -1.0 / D, -EPS,
                                            op0=ALU.mult, op1=ALU.add), "v")
                    v.wait_ge(dep.sem["v"], dep.cnt["v"])
                    t = dep.inc(v.scalar_tensor_tensor(
                        ln[:, ln_o + 4:ln_o + 5], ln[:, ln_o + 2:ln_o + 3],
                        ln[:, ln_o + 2:ln_o + 3], ln[:, ln_o + 3:ln_o + 4],
                        op0=ALU.mult, op1=ALU.add), "v")
                    dep.did_w(t, b_lnc)

                V(l1)

                def l2(s, ln_o=ln_o, b_lnc=b_lnc):
                    dep.rd(s, "s", b_lnc)
                    t = dep.inc(s.activation(ln[:, ln_o + 5:ln_o + 6],
                                             ln[:, ln_o + 4:ln_o + 5],
                                             AF.Ln, scale=-1.0), "s")
                    s.wait_ge(dep.sem["s"], t[1])
                    t = dep.inc(s.activation(ln[:, ln_o + 6:ln_o + 7],
                                             ln[:, ln_o + 5:ln_o + 6],
                                             AF.Exp, scale=-0.5), "s")
                    dep.did_w(t, b_lnc)

                S(l2)

                def l3(v, ln_o=ln_o, b_lnc=b_lnc):
                    dep.rd(v, "v", b_lnc)
                    t = dep.inc(v.scalar_tensor_tensor(
                        ln[:, ln_o + 7:ln_o + 8], ln[:, ln_o + 2:ln_o + 3],
                        -1.0, ln[:, ln_o + 6:ln_o + 7],
                        op0=ALU.mult, op1=ALU.mult), "v")
                    dep.did_w(t, b_lnc)

                V(l3)

                def l4(s, sl=sl, ln_o=ln_o, b_lnc=b_lnc):
                    dep.rd(s, "s", b_lnc, b_x1[sl])
                    dep.wr(s, "s", b_xn2[sl])
                    t = dep.inc(s.activation(xn2[sl][:, :], x1[sl][:, :],
                                             AF.Identity,
                                             bias=ln[:, ln_o + 7:ln_o + 8],
                                             scale=ln[:, ln_o + 6:ln_o + 7]),
                                "s")
                    dep.did_w(t, b_xn2[sl])
                    dep.did_r(t, b_x1[sl], b_lnc)

                S(l4)

                for b4 in range(4):
                    slot = b4 % 2

                    def xtr(t_, sl=sl, b4=b4, slot=slot):
                        dep.rd(t_, "t", b_xn2[sl], b_consts)
                        dep.wr(t_, "t", b_psB[slot])
                        for b in range(4):
                            jt = b4 * 4 + b
                            li = t_.matmul(psB[:, slot * 512 + b * 128:
                                               slot * 512 + (b + 1) * 128],
                                           xn2[sl][:, jt * 128:(jt + 1) * 128],
                                           idb[:, :], start=True, stop=True)
                        tok = dep.inc(li, "t")
                        dep.did_w(tok, b_psB[slot])
                        dep.did_r(tok, b_xn2[sl])

                    T(xtr)

                    def xev(s, b4=b4, slot=slot):
                        dep.rd(s, "s", b_psB[slot])
                        dep.wr(s, "s", b_x2T)
                        tok = dep.inc(s.copy(
                            x2T[:, b4 * 4:(b4 + 1) * 4, :],
                            psB[:, slot * 512:(slot + 1) * 512]
                            .rearrange("p (a n) -> p a n", n=128)), "s")
                        b_x2T.w = tok
                        dep.did_r(tok, b_psB[slot])

                    S(xev)

                def logits(t_):
                    dep.rd(t_, "t", b_x2T, b_consts, b_rwc)
                    dep.wr(t_, "t", b_psSp)
                    for jt in range(ET):
                        li = t_.matmul(psB[:, 1536:1536 + NEXP],
                                       x2T[:, jt, :], rw_s[:, jt, :],
                                       start=(jt == 0), stop=False)
                    li = t_.matmul(psB[:, 1536:1536 + NEXP], onesb[0:1, :],
                                   rb_s[0:1, :], start=False, stop=True)
                    tok = dep.inc(li, "t")
                    dep.did_w(tok, b_psSp)
                    dep.did_r(tok, b_x2T)

                T(logits)

                def smax_v(v, ln_o=ln_o, b_lnc=b_lnc):
                    dep.rd(v, "v", b_psSp)
                    dep.wr(v, "v", b_lnc)
                    t0 = dep.inc(v.reduce_max(ln[:, ln_o + 8:ln_o + 9],
                                              psB[:, 1536:1536 + NEXP],
                                              axis=AX.X), "v")
                    v.wait_ge(dep.sem["v"], t0[1])
                    t = dep.inc(v.tensor_scalar(ln[:, ln_o + 9:ln_o + 10],
                                                ln[:, ln_o + 8:ln_o + 9],
                                                -1.0, None, op0=ALU.mult),
                                "v")
                    dep.did_w(t, b_lnc)
                    dep.did_r(t, b_psSp)

                V(smax_v)

                def smax_s(s, ln_o=ln_o, b_lnc=b_lnc):
                    dep.rd(s, "s", b_lnc, b_psSp)
                    dep.wr(s, "s", b_p)
                    t = dep.inc(s.activation(pp[:, :],
                                             psB[:, 1536:1536 + NEXP],
                                             AF.Exp,
                                             bias=ln[:, ln_o + 9:ln_o + 10],
                                             accum_out=ln[:, ln_o + 10:
                                                          ln_o + 11]), "s")
                    dep.did_w(t, b_p)
                    b_lnc.w = t
                    dep.did_r(t, b_psSp)

                S(smax_s)

                def topk(v, ln_o=ln_o, b_lnc=b_lnc):
                    dep.rd(v, "v", b_p, b_lnc)
                    dep.wr(v, "v", b_mp)
                    v.reciprocal(ln[:, ln_o + 11:ln_o + 12],
                                 ln[:, ln_o + 10:ln_o + 11])
                    v.max(top8[:, :], pp[:, :])
                    t = dep.inc(v.engine_nop(), "v")
                    v.wait_ge(dep.sem["v"], t[1])
                    v.match_replace(wk64[:, :], top8[:, :], pp[:, :], -1.0)
                    t = dep.inc(v.engine_nop(), "v")
                    v.wait_ge(dep.sem["v"], t[1])
                    v.tensor_tensor(msk[:, :], wk64[:, :], pp[:, :],
                                    op=ALU.not_equal)
                    t = dep.inc(v.engine_nop(), "v")
                    v.wait_ge(dep.sem["v"], t[1])
                    v.tensor_mul(mpb[:, :], msk[:, :], pp[:, :])
                    t = dep.inc(v.reduce_sum(ln[:, ln_o + 12:ln_o + 13],
                                             top8[:, :], axis=AX.X), "v")
                    b_lnc.w = t
                    dep.did_w(t, b_mp)
                    dep.did_r(t, b_p)

                V(topk)

                def mptr(t_):
                    dep.rd(t_, "t", b_mp, b_consts)
                    dep.wr(t_, "t", b_psRS)
                    li = t_.matmul(psB[0:64, 1024:1024 + 128], mpb[:, :],
                                   idb[:, :], start=True, stop=True)
                    tok = dep.inc(li, "t")
                    dep.did_w(tok, b_psRS)
                    dep.did_r(tok, b_mp)

                T(mptr)

                def mpev(s):
                    dep.rd(s, "s", b_psRS)
                    dep.wr(s, "s", b_mpT)
                    tok = dep.inc(s.copy(mpT[0:64, :],
                                         psB[0:64, 1024:1024 + 128]), "s")
                    dep.did_w(tok, b_mpT)
                    dep.did_r(tok, b_psRS)

                S(mpev)

                for n in range(NCH):
                    bb = b_psRS if n % 2 == 0 else b_psSp
                    boff = 1024 if n % 2 == 0 else 1536

                    def bmat(t_, n=n, bb=bb, boff=boff):
                        dep.rd(t_, "t", b_mpT, b_rwc)
                        dep.wr(t_, "t", bb)
                        li = t_.matmul(psB[:, boff:boff + 512], mpT[0:64, :],
                                       eb_s[0:64, n * 512:(n + 1) * 512],
                                       start=True, stop=True)
                        tok = dep.inc(li, "t")
                        dep.did_w(tok, bb)
                        dep.did_r(tok, b_mpT)

                    T(bmat)

                    def ffn_c(v, sl=sl, n=n, bb=bb, boff=boff, ln_o=ln_o,
                              b_lnc=b_lnc):
                        dep.rd(v, "v", bb, b_xn2[sl], b_lnc)
                        dep.wr(v, "v", b_xt[sl])
                        tok = dep.inc(v.scalar_tensor_tensor(
                            xt[sl][:, n * 512:(n + 1) * 512],
                            xn2[sl][:, n * 512:(n + 1) * 512],
                            ln[:, ln_o + 12:ln_o + 13],
                            psB[:, boff:boff + 512],
                            op0=ALU.mult, op1=ALU.add), "v")
                        b_xt[sl].w = tok
                        dep.did_r(tok, bb, b_xn2[sl])

                    V(ffn_c)

                def ffn_out(v, sl=sl, ln_o=ln_o, b_lnc=b_lnc):
                    dep.rd(v, "v", b_xt[sl], b_x1[sl], b_lnc)
                    dep.wr(v, "v", b_x2)
                    tok = dep.inc(v.scalar_tensor_tensor(
                        x2[:, :], xt[sl][:, :], ln[:, ln_o + 11:ln_o + 12],
                        x1[sl][:, :], op0=ALU.mult, op1=ALU.add), "v")
                    dep.did_w(tok, b_x2)
                    dep.did_r(tok, b_xt[sl], b_x1[sl], b_lnc)

                V(ffn_out)

                def ywrite(q, st=st):
                    dep.rd(q, "q", b_x2)
                    tok = dep.inc(q.dma_start(
                        out=y_h[st * 128:(st + 1) * 128, :],
                        in_=x2[:, :]), "yw0", 16)
                    dep.did_r(tok, b_x2)

                Q(ywrite)

            def fin(q):
                q.wait_ge(dep.sem["yw0"], dep.cnt["yw0"])

            Q(fin)

    return nc


def _host_prep(x, qw, qb, kvw, kvb, ow, rw, rb, eb, S_CORE):
    S_TOT = NCORE * S_CORE
    f32 = np.float32
    bf16 = ml_dtypes.bfloat16

    xs = np.ascontiguousarray(np.asarray(x, dtype=f32).reshape(S_TOT, D))
    wq_pack = np.ascontiguousarray(
        np.asarray(qw, f32).T.reshape(ET, 128, ET, 128).transpose(2, 1, 0, 3))
    kw = np.asarray(kvw, f32)[:D, :]
    vw = np.asarray(kvw, f32)[D:, :]
    wk_pack = np.ascontiguousarray(
        kw.T.reshape(ET, 128, ET, 128).transpose(2, 1, 0, 3))
    wv_mov = np.ascontiguousarray(vw.T).reshape(ET, 128, D)
    wo_mov = np.ascontiguousarray(np.asarray(ow, f32).T).reshape(ET, 128, D)
    qb_t = np.ascontiguousarray(np.asarray(qb, f32).reshape(ET, 128).T)
    kb_t = np.ascontiguousarray(np.asarray(kvb, f32)[:D].reshape(ET, 128).T)
    vb_row = np.ascontiguousarray(np.asarray(kvb, f32)[D:].reshape(1, D))
    rw_pack = np.ascontiguousarray(np.asarray(rw, f32).T).reshape(ET, 128,
                                                                  NEXP)
    rb_row = np.ascontiguousarray(np.asarray(rb, f32).reshape(1, NEXP))
    eb_a = np.ascontiguousarray(np.asarray(eb, f32))

    half = 64
    inv = 1.0 / (10000.0 ** (np.arange(half, dtype=np.float64) / half))
    pos = np.arange(S_TOT, dtype=np.float64)
    fr = pos[:, None] * inv[None, :]
    emb = np.concatenate([fr, fr], axis=1)
    cosT = np.ascontiguousarray(np.cos(emb).T.astype(f32))
    sinT = np.ascontiguousarray(np.sin(emb).T.astype(f32))
    sc = f32(1.0 / np.sqrt(128.0))
    cosq = (cosT * sc).astype(f32)
    sinq = (sinT * sc).astype(f32)

    idb = np.eye(128, dtype=bf16)
    rotm = np.zeros((128, 128), dtype=f32)
    for m in range(64):
        rotm[m + 64, m] = -1.0
        rotm[m, m + 64] = 1.0
    rotm = rotm.astype(bf16)
    onesb = np.ones((128, 128), dtype=bf16)
    onesf = np.ones((1, 1), dtype=f32)

    maps = []
    for c in range(NCORE):
        s0, s1 = c * S_CORE, (c + 1) * S_CORE
        maps.append(dict(
            x=np.ascontiguousarray(xs[s0:s1]),
            wq=wq_pack, wk=wk_pack, wv=wv_mov, wo=wo_mov,
            qb=qb_t, kb=kb_t, vb=vb_row,
            rw=rw_pack, rb=rb_row, eb=eb_a,
            cosq=np.ascontiguousarray(cosq[:, s0:s1]),
            sinq=np.ascontiguousarray(sinq[:, s0:s1]),
            cosk=np.ascontiguousarray(cosT[:, s0:s1]),
            sink=np.ascontiguousarray(sinT[:, s0:s1]),
            idb=idb, rotm=rotm, onesb=onesb, onesf=onesf,
        ))
    return maps


def kernel(L0_attn_x, L0_attn_q_w, L0_attn_q_b, L0_attn_kv_w, L0_attn_kv_b,
           L0_attn_o_w, L0_ffn_router_weight, L0_ffn_router_bias,
           L0_ffn_expert_bias):
    S_CORE = 2048
    maps = _host_prep(L0_attn_x, L0_attn_q_w, L0_attn_q_b, L0_attn_kv_w,
                      L0_attn_kv_b, L0_attn_o_w, L0_ffn_router_weight,
                      L0_ffn_router_bias, L0_ffn_expert_bias, S_CORE)
    nc = build(S_CORE=S_CORE, G_ST=4, T_SUP=4)
    res = run_bass_kernel_spmd(nc, maps, core_ids=list(range(NCORE)))
    y = np.concatenate([r["y"] for r in res.results], axis=0)
    return y.reshape(1, NCORE * S_CORE, D).astype(np.float32)


# version 07
# speedup vs baseline: 1.0289x; 1.0024x over previous
import numpy as np
import ml_dtypes
from contextlib import ExitStack

import concourse.bass as bass
import concourse.mybir as mybir
from concourse.bass_utils import run_bass_kernel_spmd

F32 = mybir.dt.float32
BF16 = mybir.dt.bfloat16
AF = mybir.ActivationFunctionType
ALU = mybir.AluOpType
AX = mybir.AxisListType

NCORE = 8
D = 2048
ET = D // 128
NEXP = 64
TOPK = 8
EPS = 1e-5


class Buf:
    __slots__ = ("w", "rs")

    def __init__(self):
        self.w = None
        self.rs = {}


class Dep:

    ENGS = ("t", "v", "s", "g", "q")

    def __init__(self, nc, ctx):
        keys = ["t", "v", "s", "cc", "gb", "qt",
                "qx0", "qx1", "gw0", "gw1", "gw2", "qk0", "qk1", "qv0", "qv1",
                "yw0", "yw1", "sp0", "sp1", "kp0", "kp1"]
        self.sem = {k: ctx.enter_context(nc.semaphore("sem_" + k)) for k in keys}
        self.cnt = {k: 0 for k in keys}
        self.seen = {e: {k: 0 for k in keys} for e in self.ENGS}

    def wait(self, eng, ename, tok):
        if tok is None:
            return
        key, v = tok
        if self.seen[ename][key] < v:
            eng.wait_ge(self.sem[key], v)
            self.seen[ename][key] = v

    def rd(self, eng, ename, *bufs):
        for b in bufs:
            self.wait(eng, ename, b.w)

    def wr(self, eng, ename, *bufs):
        for b in bufs:
            self.wait(eng, ename, b.w)
            for t in b.rs.values():
                self.wait(eng, ename, t)

    def inc(self, instr, key, n=1):
        instr.then_inc(self.sem[key], n)
        self.cnt[key] += n
        return (key, self.cnt[key])

    @staticmethod
    def did_w(tok, *bufs):
        for b in bufs:
            b.w = tok
            b.rs = {}

    @staticmethod
    def did_r(tok, *bufs):
        for b in bufs:
            b.rs[tok[0]] = tok

    def guard_merge(self, *bufs):
        if not hasattr(self, "guard"):
            self.guard = {}
        for b in bufs:
            toks = list(b.rs.values()) + ([b.w] if b.w else [])
            for k, v in toks:
                if k not in self.guard or self.guard[k][1] < v:
                    self.guard[k] = (k, v)

    def newbuf(self):
        b = Buf()
        if hasattr(self, "guard"):
            b.rs = dict(self.guard)
        return b


def build(S_CORE=2048, G_ST=4, T_SUP=4):
    S_TOT = NCORE * S_CORE
    NST = S_CORE // 128
    NG = NST // G_ST
    SW = G_ST * 128
    assert SW <= 512
    TP = T_SUP * 128
    NSUP = S_TOT // TP
    assert S_CORE % TP == 0
    SUP_PER_CORE = S_CORE // TP
    REG = ET * SW
    assert REG == G_ST * D
    NCH = D // 512

    nc = bass.Bass("TRN2", target_bir_lowering=False)

    x_h = nc.dram_tensor("x", [S_CORE, D], F32, kind="ExternalInput")
    wq_h = nc.dram_tensor("wq", [ET, 128, ET, 128], F32, kind="ExternalInput")
    wk_h = nc.dram_tensor("wk", [ET, 128, ET, 128], F32, kind="ExternalInput")
    wv_h = nc.dram_tensor("wv", [ET, 128, D], F32, kind="ExternalInput")
    wo_h = nc.dram_tensor("wo", [ET, 128, D], F32, kind="ExternalInput")
    qb_h = nc.dram_tensor("qb", [128, ET], F32, kind="ExternalInput")
    kb_h = nc.dram_tensor("kb", [128, ET], F32, kind="ExternalInput")
    vb_h = nc.dram_tensor("vb", [1, D], F32, kind="ExternalInput")
    rw_h = nc.dram_tensor("rw", [ET, 128, NEXP], F32, kind="ExternalInput")
    rb_h = nc.dram_tensor("rb", [1, NEXP], F32, kind="ExternalInput")
    eb_h = nc.dram_tensor("eb", [NEXP, D], F32, kind="ExternalInput")
    cosq_h = nc.dram_tensor("cosq", [128, S_CORE], F32, kind="ExternalInput")
    sinq_h = nc.dram_tensor("sinq", [128, S_CORE], F32, kind="ExternalInput")
    cosk_h = nc.dram_tensor("cosk", [128, S_CORE], F32, kind="ExternalInput")
    sink_h = nc.dram_tensor("sink", [128, S_CORE], F32, kind="ExternalInput")
    idb_h = nc.dram_tensor("idb", [128, 128], BF16, kind="ExternalInput")
    rotm_h = nc.dram_tensor("rotm", [128, 128], BF16, kind="ExternalInput")
    onesb_h = nc.dram_tensor("onesb", [128, 128], BF16, kind="ExternalInput")
    onesf_h = nc.dram_tensor("onesf", [1, 1], F32, kind="ExternalInput")
    y_h = nc.dram_tensor("y", [S_CORE, D], F32, kind="ExternalOutput")

    KVN = D * S_CORE
    kv_in = nc.dram_tensor("kv_in", [2 * KVN], BF16, kind="Internal")
    k_out = nc.dram_tensor("k_out", [NCORE * KVN], BF16, kind="Internal",
                           addr_space="Shared")
    v_out = nc.dram_tensor("v_out", [NCORE * KVN], BF16, kind="Internal",
                           addr_space="Shared")

    with ExitStack() as ctx:
        en = ctx.enter_context
        dep = Dep(nc, ctx)
        block = en(nc.Block())

        def T(f):
            block.tensor(f)

        def V(f):
            block.vector(f)

        def S(f):
            block.scalar(f)

        def Gp(f):
            block.gpsimd(f)

        def Q(f):
            block.sync(f)

        idb = en(nc.sbuf_tensor("idb_s", [128, 128], BF16))
        rotm = en(nc.sbuf_tensor("rotm_s", [128, 128], BF16))
        onesb = en(nc.sbuf_tensor("onesb_s", [128, 128], BF16))
        onesf = en(nc.sbuf_tensor("onesf_s", [1, 1], F32))
        qb_s = en(nc.sbuf_tensor("qb_s", [128, ET], F32))
        kb_s = en(nc.sbuf_tensor("kb_s", [128, ET], F32))
        xt = [en(nc.sbuf_tensor(f"xt{i}", [128, D], F32)) for i in range(2)]
        b_wo_c0 = Buf()
        ln = en(nc.sbuf_tensor("ln", [128, 32], F32))

        b_xt = [Buf(), Buf()]
        b_ln = Buf()
        b_qTg = [Buf() for _ in range(NG)]
        b_kvin = Buf()
        b_consts = Buf()
        b_psA = Buf()
        b_psAh = [Buf(), Buf()]
        b_psB = [Buf(), Buf()]
        b_psRS = Buf()
        b_psSp = Buf()

        psA = en(nc.psum_tensor("psA", [128, 2048], F32))
        psB = en(nc.psum_tensor("psB", [128, 2048], F32))

        def consts(g):
            pairs = [(idb, idb_h), (rotm, rotm_h), (onesb, onesb_h),
                     (onesf, onesf_h), (qb_s, qb_h), (kb_s, kb_h)]
            t = None
            for dst, src in pairs:
                g.wait_ge(dep.sem["gb"], dep.cnt["gb"])
                t = dep.inc(g.dma_start(out=dst[:, :], in_=src[:, :]),
                            "gb", 16)
            dep.did_w(t, b_consts)

        Gp(consts)

        def preload_tbl(sc):
            from concourse.hw_specs import get_activation_tables
            for idx, nm in enumerate(get_activation_tables(nc.m.arch)):
                if nm == "natural_log_exp_and_others":
                    sc.add_instruction(mybir.InstLoadActFuncSet(
                        name=nc.get_next_instruction_name(),
                        act_func_set_id=idx, ins=[], outs=[]))
                    return
            raise ValueError("no natural_log_exp_and_others table set")

        S(preload_tbl)

        c12 = ctx.enter_context(ExitStack())
        xnT = c12.enter_context(nc.sbuf_tensor("xnT", [128, ET, S_CORE],
                                               BF16))
        b_xnT = Buf()
        with ExitStack() as c1:
            wv_s = c1.enter_context(nc.sbuf_tensor("wv_s", [128, ET, D], BF16))
            vb_s = c1.enter_context(nc.sbuf_tensor("vb_s", [1, D], BF16))
            xnt = [c1.enter_context(nc.sbuf_tensor(f"xnt{i}", [128, D], BF16))
                   for i in range(2)]
            scr = c1.enter_context(nc.sbuf_tensor("scr", [128, D], F32))
            b_xnt = [Buf(), Buf()]
            b_scr = Buf()
            vst = [c1.enter_context(nc.sbuf_tensor(f"vst{i}", [128, D], BF16))
                   for i in range(2)]
            b_wv = [Buf() for _ in range(4)]
            b_vst = [Buf(), Buf()]

            def wv_load(g):
                g.wait_ge(dep.sem["gb"], dep.cnt["gb"])
                dep.inc(g.dma_start(out=vb_s[:, :], in_=vb_h[:, :]), "gb", 16)
                for c in range(4):
                    g.wait_ge(dep.sem["gb"], dep.cnt["gb"])
                    t = dep.inc(g.dma_start(
                        out=wv_s[:, c * 4:(c + 1) * 4, :],
                        in_=wv_h[c * 4:(c + 1) * 4, :, :]
                        .rearrange("a p n -> p a n")), "gb", 16)
                    dep.did_w(t, b_wv[c])

            Gp(wv_load)

            def xload(q, st, sl):
                dep.wr(q, "q", b_xt[sl])
                t = dep.inc(q.dma_start(out=xt[sl][:, :],
                                        in_=x_h[st * 128:(st + 1) * 128, :]),
                            "qx%d" % sl, 16)
                dep.did_w(t, b_xt[sl])

            Q(lambda q: xload(q, 0, 0))

            for st in range(NST):
                sl = st % 2
                if st + 1 < NST:
                    Q(lambda q, st=st, sl=sl: xload(q, st + 1, 1 - sl))

                def ln1(v, sl=sl):
                    dep.rd(v, "v", b_xt[sl])
                    dep.wr(v, "v", b_ln, b_scr)
                    v.reduce_sum(ln[:, 0:1], xt[sl][:, :], axis=AX.X)
                    t = dep.inc(v.scalar_tensor_tensor(
                        scr[:, :], xt[sl][:, :], 1.0, xt[sl][:, :],
                        op0=ALU.mult, op1=ALU.mult, accum_out=ln[:, 1:2]), "v")
                    dep.did_w(t, b_scr)
                    dep.did_r(t, b_xt[sl])
                    v.wait_ge(dep.sem["v"], dep.cnt["v"])
                    v.tensor_scalar(ln[:, 2:3], ln[:, 0:1], 1.0 / D, None,
                                    op0=ALU.mult)
                    t = dep.inc(v.tensor_scalar(ln[:, 3:4], ln[:, 1:2],
                                                -1.0 / D, -EPS,
                                                op0=ALU.mult, op1=ALU.add), "v")
                    v.wait_ge(dep.sem["v"], dep.cnt["v"])
                    t = dep.inc(v.scalar_tensor_tensor(
                        ln[:, 4:5], ln[:, 2:3], ln[:, 2:3], ln[:, 3:4],
                        op0=ALU.mult, op1=ALU.add), "v")
                    dep.did_w(t, b_ln)

                V(ln1)

                def ln2(s, sl=sl):
                    dep.rd(s, "s", b_ln)
                    t = dep.inc(s.activation(ln[:, 5:6], ln[:, 4:5],
                                             AF.Ln, scale=-1.0), "s")
                    s.wait_ge(dep.sem["s"], t[1])
                    t = dep.inc(s.activation(ln[:, 6:7], ln[:, 5:6], AF.Exp,
                                             scale=-0.5), "s")
                    dep.did_w(t, b_ln)

                S(ln2)

                def ln3(v, sl=sl):
                    dep.rd(v, "v", b_ln)
                    t = dep.inc(v.scalar_tensor_tensor(
                        ln[:, 7:8], ln[:, 2:3], -1.0, ln[:, 6:7],
                        op0=ALU.mult, op1=ALU.mult), "v")
                    dep.did_w(t, b_ln)

                V(ln3)

                def ln4(s, sl=sl):
                    dep.rd(s, "s", b_ln, b_xt[sl])
                    dep.wr(s, "s", b_xnt[sl])
                    t = dep.inc(s.activation(xnt[sl][:, :], xt[sl][:, :],
                                             AF.Identity, bias=ln[:, 7:8],
                                             scale=ln[:, 6:7]), "s")
                    dep.did_w(t, b_xnt[sl])
                    dep.did_r(t, b_xt[sl], b_ln)

                S(ln4)

                for b4 in range(4):
                    slot = b4 % 2

                    def tr(t_, st=st, sl=sl, b4=b4, slot=slot):
                        dep.rd(t_, "t", b_xnt[sl], b_consts)
                        dep.wr(t_, "t", b_psB[slot])
                        for b in range(4):
                            jt = b4 * 4 + b
                            li = t_.matmul(psB[:, slot * 512 + b * 128:
                                          slot * 512 + (b + 1) * 128],
                                      xnt[sl][:, jt * 128:(jt + 1) * 128],
                                      idb[:, :], start=True, stop=True)
                        tok = dep.inc(li, "t")
                        dep.did_w(tok, b_psB[slot])
                        dep.did_r(tok, b_xnt[sl])

                    T(tr)

                    def ev(s, st=st, b4=b4, slot=slot):
                        dep.rd(s, "s", b_psB[slot])
                        dep.wr(s, "s", b_xnT)
                        tok = dep.inc(s.copy(
                            xnT[:, b4 * 4:(b4 + 1) * 4,
                                st * 128:(st + 1) * 128],
                            psB[:, slot * 512:(slot + 1) * 512]
                            .rearrange("p (a n) -> p a n", n=128)), "s")
                        b_xnT.w = tok
                        dep.did_r(tok, b_psB[slot])

                    S(ev)

                def vproj(t_, st=st):
                    dep.rd(t_, "t", b_xnT, b_consts)
                    dep.wr(t_, "t", b_psA)
                    for jt in range(ET):
                        dep.rd(t_, "t", b_wv[jt // 4])
                        for n in range(NCH):
                            li = t_.matmul(psA[:, n * 512:(n + 1) * 512],
                                      xnT[:, jt, st * 128:(st + 1) * 128],
                                      wv_s[:, jt, n * 512:(n + 1) * 512],
                                      start=(jt == 0), stop=False)
                    for n in range(NCH):
                        li = t_.matmul(psA[:, n * 512:(n + 1) * 512],
                                  onesb[0:1, :], vb_s[0:1, n * 512:(n + 1) * 512],
                                  start=False, stop=True)
                    tok = dep.inc(li, "t")
                    dep.did_w(tok, b_psA)
                    dep.did_r(tok, b_xnT)
                    for c in range(4):
                        dep.did_r(tok, b_wv[c])

                T(vproj)

                def vev(s, st=st, sl=sl):
                    dep.rd(s, "s", b_psA)
                    dep.wr(s, "s", b_vst[sl])
                    tok = dep.inc(s.copy(vst[sl][:, :], psA[:, :]), "s")
                    dep.did_w(tok, b_vst[sl])
                    dep.did_r(tok, b_psA)

                S(vev)

                def vpub(s, st=st, sl=sl):
                    dep.rd(s, "s", b_vst[sl])
                    off = st * 128 * D
                    tok = dep.inc(s.dma_start(
                        out=kv_in[off:off + 128 * D]
                        .rearrange("(p n) -> p n", p=128),
                        in_=vst[sl][:, :]), "sp%d" % sl, 16)
                    dep.did_r(tok, b_vst[sl], b_kvin)

                S(vpub)

            dep.guard_merge(*b_wv, b_vst[0], b_vst[1], b_psA, b_xnt[0],
                            b_xnt[1])

        b_kvoutv = Buf()
        b_kvout = Buf()

        def ag_v(g):
            for t in list(b_kvin.rs.values()):
                dep.wait(g, "g", t)
            tok = dep.inc(g.collective_compute(
                "AllGather", ALU.bypass,
                replica_groups=[list(range(NCORE))],
                ins=[kv_in[0:KVN].opt()],
                outs=[v_out[:].opt()]), "cc")
            dep.did_w(tok, b_kvoutv)

        qT = ctx.enter_context(nc.sbuf_tensor("qT", [128, NG * REG], BF16,
                                              side="right"))

        with ExitStack() as c2:
            csq = c2.enter_context(nc.sbuf_tensor("csq", [128, S_CORE], F32))
            snq = c2.enter_context(nc.sbuf_tensor("snq", [128, S_CORE], F32))
            wcol = [c2.enter_context(
                nc.sbuf_tensor(f"wcol{i}", [128, ET, 128], BF16))
                for i in range(3)]
            tq = [c2.enter_context(nc.sbuf_tensor(f"tq{i}", [128, S_CORE],
                                                  BF16)) for i in range(2)]
            m1 = c2.enter_context(nc.sbuf_tensor("m1", [128, S_CORE], BF16))
            m2 = c2.enter_context(nc.sbuf_tensor("m2", [128, S_CORE], BF16))
            kst = [c2.enter_context(nc.sbuf_tensor(f"kst{i}", [128, S_CORE],
                                                   BF16)) for i in range(2)]
            b_tab = dep.newbuf()
            b_wcol = [dep.newbuf(), dep.newbuf(), dep.newbuf()]
            b_tq = [dep.newbuf(), dep.newbuf()]
            b_m = dep.newbuf()
            b_kst = [dep.newbuf(), dep.newbuf()]

            def tabs(q, ch, sh):
                dep.wr(q, "q", b_tab)
                dep.inc(q.dma_start(out=csq[:, :], in_=ch[:, :]), "qt", 16)
                t = dep.inc(q.dma_start(out=snq[:, :], in_=sh[:, :]),
                            "qt", 16)
                dep.did_w(t, b_tab)

            Q(lambda q: tabs(q, cosk_h, sink_h))

            def wload(g, it, sl, wh):
                dep.wr(g, "g", b_wcol[sl])
                t = dep.inc(g.dma_start(
                    out=wcol[sl][:, :, :],
                    in_=wh[it, :, :, :]), "gw%d" % sl, 16)
                dep.did_w(t, b_wcol[sl])

            Gp(lambda g: wload(g, 0, 0, wk_h))
            Gp(lambda g: wload(g, 1, 1, wk_h))

            W1 = min(512, S_CORE)
            NSC = S_CORE // W1
            for it in range(2 * ET):
                isq = it >= ET
                et = it % ET
                sl = it % 3
                p2 = it % 2
                if it == ET:
                    Q(lambda q: tabs(q, cosq_h, sinq_h))
                if it == 8:
                    Gp(ag_v)
                if it == 20:
                    def ag_k(g):
                        for t in list(b_kvin.rs.values()):
                            dep.wait(g, "g", t)
                        tok = dep.inc(g.collective_compute(
                            "AllGather", ALU.bypass,
                            replica_groups=[list(range(NCORE))],
                            ins=[kv_in[KVN:2 * KVN].opt()],
                            outs=[k_out[:].opt()]), "cc")
                        dep.did_w(tok, b_kvout)

                    Gp(ag_k)
                if it + 2 < 2 * ET:
                    nxt = it + 2
                    Gp(lambda g, nxt=nxt: wload(
                        g, nxt % ET, nxt % 3, wk_h if nxt < ET else wq_h))

                pt = psA if it % 2 == 0 else psB
                pbufs = ([b_psA] if it % 2 == 0 else
                         [b_psB[0], b_psB[1], b_psRS, b_psSp])

                def proj(t_, it=it, sl=sl, pt=pt, pbufs=pbufs):
                    dep.rd(t_, "t", b_wcol[sl], b_xnT)
                    for _b in pbufs:
                        dep.wr(t_, "t", _b)
                    for jt in range(ET):
                        for n in range(NSC):
                            li = t_.matmul(pt[:, n * W1:(n + 1) * W1],
                                      wcol[sl][:, jt, :],
                                      xnT[:, jt, n * W1:(n + 1) * W1],
                                      start=(jt == 0), stop=(jt == ET - 1))
                    tok = dep.inc(li, "t")
                    dep.did_w(tok, *pbufs)
                    dep.did_r(tok, b_wcol[sl], b_xnT)

                T(proj)

                def bias(s, it=it, isq=isq, et=et, pt=pt, pbufs=pbufs,
                         sl=p2):
                    for _b in pbufs:
                        dep.rd(s, "s", _b)
                    dep.wr(s, "s", b_tq[sl])
                    bb = qb_s if isq else kb_s
                    tok = dep.inc(s.activation(tq[sl][:, :], pt[:, 0:S_CORE],
                                               AF.Identity,
                                               bias=bb[:, et:et + 1]), "s")
                    dep.did_w(tok, b_tq[sl])
                    for _b in pbufs:
                        dep.did_r(tok, _b)

                S(bias)

                def rot(t_, it=it, pt=pt, pbufs=pbufs, sl=p2):
                    dep.rd(t_, "t", b_tq[sl], b_consts)
                    for _b in pbufs:
                        dep.wr(t_, "t", _b)
                    for n in range(NSC):
                        li = t_.matmul(pt[:, n * W1:(n + 1) * W1], rotm[:, :],
                                  tq[sl][:, n * W1:(n + 1) * W1],
                                  start=True, stop=True)
                    tok = dep.inc(li, "t")
                    dep.did_w(tok, *pbufs)
                    dep.did_r(tok, b_tq[sl])

                T(rot)

                def rope(v, it=it, isq=isq, et=et, sl=p2, pt=pt,
                         pbufs=pbufs, ksl=p2):
                    cs, sn = csq, snq
                    dep.rd(v, "v", b_tq[sl], b_tab)
                    for _b in pbufs:
                        dep.rd(v, "v", _b)
                    dep.wr(v, "v", b_m)
                    v.tensor_mul(m1[:, :], tq[sl][:, :], cs[:, :])
                    tok = dep.inc(v.tensor_mul(m2[:, :], pt[:, 0:S_CORE],
                                               sn[:, :]), "v")
                    dep.did_w(tok, b_m)
                    dep.did_r(tok, b_tq[sl], b_tab)
                    for _b in pbufs:
                        dep.did_r(tok, _b)
                    v.wait_ge(dep.sem["v"], tok[1])
                    if isq:
                        for g in range(NG):
                            dep.wr(v, "v", b_qTg[g])
                        dst = qT[:, :].rearrange("p (g r) -> p g r", g=NG)[
                            :, :, et * SW:(et + 1) * SW]
                        src1 = m1[:, :].rearrange("p (g r) -> p g r", g=NG)
                        src2 = m2[:, :].rearrange("p (g r) -> p g r", g=NG)
                        tok = dep.inc(v.tensor_add(dst, src1, src2), "v")
                        for g in range(NG):
                            dep.did_w(tok, b_qTg[g])
                    else:
                        dep.wr(v, "v", b_kst[sl])
                        tok = dep.inc(v.tensor_add(kst[sl][:, :], m1[:, :],
                                                   m2[:, :]), "v")
                        dep.did_w(tok, b_kst[sl])
                    dep.did_r(tok, b_m)

                V(rope)

                if not isq:
                    def kpub(s, et=et, sl=p2):
                        dep.rd(s, "s", b_kst[sl])
                        off = KVN + et * 128 * S_CORE
                        tok = dep.inc(s.dma_start(
                            out=kv_in[off:off + 128 * S_CORE]
                            .rearrange("(p n) -> p n", p=128),
                            in_=kst[sl][:, :]), "kp%d" % sl, 16)
                        dep.did_r(tok, b_kst[sl], b_kvin)

                    S(kpub)

            dep.guard_merge(b_tab, b_wcol[0], b_wcol[1], b_tq[0], b_tq[1],
                            b_m, b_kst[0], b_kst[1])
        dep.guard_merge(b_xnT)
        c12.close()


        wo_c0 = ctx.enter_context(nc.sbuf_tensor("wo_c0", [128, ET, 512],
                                                 BF16, side="right"))

        with ExitStack() as c3:
            kbuf = [c3.enter_context(nc.sbuf_tensor(f"kbuf{i}", [128, ET, TP],
                                                    BF16)) for i in range(2)]
            vbuf = [c3.enter_context(nc.sbuf_tensor(f"vbuf{i}",
                                                    [128, T_SUP, D], BF16))
                    for i in range(2)]
            expT = [c3.enter_context(nc.sbuf_tensor(f"expT{i}",
                                                    [128, T_SUP, SW], BF16))
                    for i in range(2)]
            o_acc = c3.enter_context(nc.sbuf_tensor("o_acc",
                                                    [128, G_ST, D], F32))
            rrow = c3.enter_context(nc.sbuf_tensor("rrow", [1, SW], F32))
            rrec = c3.enter_context(nc.sbuf_tensor("rrec", [1, SW], F32))
            rct = c3.enter_context(nc.sbuf_tensor("rct", [128, G_ST], F32))
            def _tmax(a, b):
                if a is None:
                    return b
                if b is None:
                    return a
                return a if a[1] >= b[1] else b

            b_psAh[0].w = _tmax(b_psAh[0].w, b_psA.w)
            b_psAh[1].w = _tmax(b_psAh[1].w, b_psA.w)
            for _k, _t in b_psA.rs.items():
                b_psAh[0].rs[_k] = _t
                b_psAh[1].rs[_k] = _t
            b_kb = [dep.newbuf(), dep.newbuf()]
            b_vb = [dep.newbuf(), dep.newbuf()]
            b_ex = [dep.newbuf(), dep.newbuf()]
            b_oac = [dep.newbuf() for _ in range(G_ST)]
            b_r = dep.newbuf()
            b_rct = dep.newbuf()

            def kload(q, su_idx):
                su = su_idx % NSUP
                ks = su_idx % 2
                c = (su * TP) // S_CORE
                r0 = (su * TP) % S_CORE
                base = c * KVN
                dep.rd(q, "q", b_kvout)
                dep.wr(q, "q", b_kb[ks])
                tok = dep.inc(q.dma_start(
                    out=kbuf[ks][:, :, :],
                    in_=k_out[base:base + KVN]
                    .rearrange("(a p s) -> p a s", p=128, s=S_CORE)
                    [:, :, r0:r0 + TP]), "qk%d" % ks, 16)
                dep.did_w(tok, b_kb[ks])

            def vload(q, su_idx):
                su = su_idx % NSUP
                vs = su_idx % 2
                c = (su * TP) // S_CORE
                r0 = (su * TP) % S_CORE
                base = c * KVN
                dep.rd(q, "q", b_kvoutv)
                dep.wr(q, "q", b_vb[vs])
                vb0 = base + r0 * D
                tok = dep.inc(q.dma_start(
                    out=vbuf[vs][:, :, :],
                    in_=v_out[vb0:vb0 + TP * D]
                    .rearrange("(a p e) -> p a e", p=128, e=D)), "qv%d" % vs,
                    16)
                dep.did_w(tok, b_vb[vs])

            Q(lambda q: kload(q, 0))
            Q(lambda q: vload(q, 0))

            def wo_c0_load(g):
                g.wait_ge(dep.sem["gb"], dep.cnt["gb"])
                tok = dep.inc(g.dma_start(
                    out=wo_c0[:, :, :],
                    in_=wo_h[:, :, 0:512].rearrange("a p n -> p a n")),
                    "gb", 16)
                dep.did_w(tok, b_wo_c0)

            Gp(wo_c0_load)

            def scores_block(g, ssu, su_idx):
                ks = su_idx % 2
                ex = su_idx % 2
                for tt in range(T_SUP):
                    pb = tt % 2

                    def sc(t_, g=g, tt=tt, ks=ks, pb=pb):
                        dep.rd(t_, "t", b_kb[ks], b_qTg[g])
                        dep.wr(t_, "t", b_psB[pb])
                        for et in range(ET):
                            li = t_.matmul(psB[:, pb * 512:pb * 512 + SW],
                                           kbuf[ks][:, et,
                                                    tt * 128:(tt + 1) * 128],
                                           qT[:, g * REG + et * SW:
                                              g * REG + (et + 1) * SW],
                                           start=(et == 0),
                                           stop=(et == ET - 1))
                        tok = dep.inc(li, "t")
                        dep.did_w(tok, b_psB[pb])
                        dep.did_r(tok, b_kb[ks], b_qTg[g])

                    T(sc)

                    def ex_(s, tt=tt, ex=ex, pb=pb):
                        dep.rd(s, "s", b_psB[pb])
                        dep.wr(s, "s", b_ex[ex])
                        tok = dep.inc(s.activation(
                            expT[ex][:, tt, :],
                            psB[:, pb * 512:pb * 512 + SW], AF.Exp), "s")
                        b_ex[ex].w = tok
                        dep.did_r(tok, b_psB[pb])

                    S(ex_)

            def pv_block(g, ssu, su_idx):
                ex = su_idx % 2
                vs = su_idx % 2

                def rsum(t_, g=g, ssu=ssu, ex=ex):
                    dep.rd(t_, "t", b_ex[ex], b_consts)
                    if ssu == 0:
                        dep.wr(t_, "t", b_psRS)
                    for tt in range(T_SUP):
                        li = t_.matmul(psB[0:1, 1024:1024 + SW],
                                       onesb[:, 0:1], expT[ex][:, tt, :],
                                       start=(ssu == 0 and tt == 0),
                                       stop=(ssu == NSUP - 1 and
                                             tt == T_SUP - 1))
                    tok = dep.inc(li, "t")
                    dep.did_w(tok, b_psRS)
                    dep.did_r(tok, b_ex[ex])

                T(rsum)

                for stl in range(G_ST):
                    def pv(t_, ssu=ssu, vs=vs, ex=ex, stl=stl):
                        dep.rd(t_, "t", b_ex[ex], b_vb[vs])
                        for h in range(2):
                            dep.wr(t_, "t", b_psAh[h])
                            for tt in range(T_SUP):
                                for n in (0, 1):
                                    nn = h * 2 + n
                                    li = t_.matmul(
                                        psA[:, nn * 512:(nn + 1) * 512],
                                        expT[ex][:, tt,
                                                 stl * 128:(stl + 1) * 128],
                                        vbuf[vs][:, tt,
                                                 nn * 512:(nn + 1) * 512],
                                        start=(tt == 0),
                                        stop=(tt == T_SUP - 1))
                            tok = dep.inc(li, "t")
                            dep.did_w(tok, b_psAh[h])
                            dep.did_r(tok, b_ex[ex], b_vb[vs])

                    T(pv)

                    def fl(v, ssu=ssu, stl=stl):
                        for h in range(2):
                            dep.rd(v, "v", b_psAh[h])
                            dep.wr(v, "v", b_oac[stl])
                            dst = o_acc[:, stl, h * 1024:(h + 1) * 1024]
                            if ssu == 0:
                                ins = v.tensor_copy(
                                    dst, psA[:, h * 1024:(h + 1) * 1024])
                            else:
                                ins = v.tensor_add(
                                    dst, dst,
                                    psA[:, h * 1024:(h + 1) * 1024])
                            tok = dep.inc(ins, "v")
                            dep.did_w(tok, b_oac[stl])
                            dep.did_r(tok, b_psAh[h])

                    V(fl)

            def sweep_end(g):
                def rfin_s(s, g=g):
                    dep.rd(s, "s", b_psRS)
                    dep.wr(s, "s", b_r)
                    tok = dep.inc(s.copy(rrow[0:1, :],
                                         psB[0:1, 1024:1024 + SW]), "s")
                    dep.did_w(tok, b_r)
                    dep.did_r(tok, b_psRS)

                S(rfin_s)

                def rfin_v(v, g=g):
                    dep.rd(v, "v", b_r)
                    tok = dep.inc(v.reciprocal(rrec[0:1, :], rrow[0:1, :]),
                                  "v")
                    dep.did_w(tok, b_r)

                V(rfin_v)

                def rtr(t_, g=g):
                    dep.rd(t_, "t", b_r, b_consts)
                    dep.wr(t_, "t", b_psSp)
                    for stl in range(G_ST):
                        li = t_.matmul(psB[:, 1536 + stl:1536 + stl + 1],
                                       rrec[0:1, stl * 128:(stl + 1) * 128],
                                       onesf[0:1, 0:1], start=True, stop=True)
                    tok = dep.inc(li, "t")
                    dep.did_w(tok, b_psSp)
                    dep.did_r(tok, b_r)

                T(rtr)

                def rev(s, g=g):
                    dep.rd(s, "s", b_psSp)
                    dep.wr(s, "s", b_rct)
                    tok = dep.inc(s.copy(rct[:, :],
                                         psB[:, 1536:1536 + G_ST]), "s")
                    dep.did_w(tok, b_rct)
                    dep.did_r(tok, b_psSp)

                S(rev)

                for stl in range(G_ST):
                    def obf(v, g=g, stl=stl):
                        dep.rd(v, "v", b_oac[stl], b_rct)
                        dep.wr(v, "v", b_qTg[g])
                        dst = qT[:, g * REG + stl * D:g * REG + (stl + 1) * D]
                        tok = dep.inc(v.tensor_scalar(
                            dst, o_acc[:, stl, :], rct[:, stl:stl + 1], None,
                            op0=ALU.mult), "v")
                        b_qTg[g].w = tok
                        dep.did_r(tok, b_oac[stl], b_rct)

                    V(obf)

            NSU_ALL = NG * NSUP
            for su_idx in range(NSU_ALL + 1):
                if su_idx < NSU_ALL:
                    g, ssu = divmod(su_idx, NSUP)
                    if su_idx + 1 < NSU_ALL:
                        Q(lambda q, si=su_idx + 1: kload(q, si))
                    scores_block(g, ssu, su_idx)
                if su_idx >= 1:
                    pg, pssu = divmod(su_idx - 1, NSUP)
                    pv_block(pg, pssu, su_idx - 1)
                    if pssu == NSUP - 1:
                        sweep_end(pg)
                if 1 <= su_idx + 1 < NSU_ALL:
                    Q(lambda q, si=su_idx + 1: vload(q, si))

            dep.guard_merge(b_kb[0], b_kb[1], b_vb[0], b_vb[1], b_ex[0],
                            b_ex[1], *b_oac, b_r, b_rct)

        def _tmax2(a, b):
            if a is None:
                return b
            if b is None:
                return a
            return a if a[1] >= b[1] else b

        b_psA.w = _tmax2(_tmax2(b_psA.w, b_psAh[0].w), b_psAh[1].w)
        for _h in (0, 1):
            for _k, _t in b_psAh[_h].rs.items():
                if _k not in b_psA.rs or b_psA.rs[_k][1] < _t[1]:
                    b_psA.rs[_k] = _t

        with ExitStack() as c4:
            wo_s = c4.enter_context(nc.sbuf_tensor("wo_s", [128, ET, 3 * 512],
                                                   BF16))
            rw_s = c4.enter_context(nc.sbuf_tensor("rw_s", [128, ET, NEXP],
                                                   BF16))
            rb_s = c4.enter_context(nc.sbuf_tensor("rb_s", [1, NEXP], BF16))
            eb_s = c4.enter_context(nc.sbuf_tensor("eb_s", [NEXP, D], BF16))
            scr = c4.enter_context(nc.sbuf_tensor("scr2", [128, D], F32))
            b_scr = dep.newbuf()
            oT = [c4.enter_context(nc.sbuf_tensor(f"oT{i}", [128, ET, 128],
                                                  BF16)) for i in range(2)]
            x2T = c4.enter_context(nc.sbuf_tensor("x2T", [128, ET, 128],
                                                  BF16))
            x1 = [c4.enter_context(nc.sbuf_tensor(f"x1_{i}", [128, D], F32))
                  for i in range(2)]
            xn2 = [c4.enter_context(nc.sbuf_tensor(f"xn2_{i}", [128, D],
                                                   BF16)) for i in range(2)]
            x2 = c4.enter_context(nc.sbuf_tensor("x2_", [128, D], F32))
            pp = c4.enter_context(nc.sbuf_tensor("pp", [128, NEXP], F32))
            wk64 = c4.enter_context(nc.sbuf_tensor("wk64", [128, NEXP], F32))
            top8 = c4.enter_context(nc.sbuf_tensor("top8", [128, TOPK], F32))
            msk = c4.enter_context(nc.sbuf_tensor("msk", [128, NEXP], F32))
            mpb = c4.enter_context(nc.sbuf_tensor("mpb", [128, NEXP], BF16))
            mpT = c4.enter_context(nc.sbuf_tensor("mpT", [64, 128], BF16))
            b_wo = [dep.newbuf() for _ in range(NCH)]
            b_rwc = dep.newbuf()
            b_oT = [dep.newbuf(), dep.newbuf()]
            b_x2T = dep.newbuf()
            b_x1 = [dep.newbuf(), dep.newbuf()]
            b_xn2 = [dep.newbuf(), dep.newbuf()]
            b_x2 = dep.newbuf()
            b_p = dep.newbuf()
            b_mp = dep.newbuf()
            b_mpT = dep.newbuf()
            b_lnp = [b_ln, dep.newbuf()]

            def wo_load(g):
                dep.wr(g, "g", b_rwc)
                g.wait_ge(dep.sem["gb"], dep.cnt["gb"])
                dep.inc(g.dma_start(
                    out=rw_s[:, :, :],
                    in_=rw_h[:, :, :].rearrange("a p n -> p a n")), "gb", 16)
                g.wait_ge(dep.sem["gb"], dep.cnt["gb"])
                dep.inc(g.dma_start(out=rb_s[:, :], in_=rb_h[:, :]), "gb", 16)
                g.wait_ge(dep.sem["gb"], dep.cnt["gb"])
                tok = dep.inc(g.dma_start(out=eb_s[:, :], in_=eb_h[:, :]),
                              "gb", 16)
                dep.did_w(tok, b_rwc)
                for n in range(1, NCH):
                    dep.wr(g, "g", b_wo[n])
                    g.wait_ge(dep.sem["gb"], dep.cnt["gb"])
                    tok = dep.inc(g.dma_start(
                        out=wo_s[:, :, (n - 1) * 512:n * 512],
                        in_=wo_h[:, :, n * 512:(n + 1) * 512]
                        .rearrange("a p n -> p a n")), "gb", 16)
                    dep.did_w(tok, b_wo[n])

            Gp(wo_load)
            Q(lambda q: xload(q, 0, 0))

            def emit_otr(st):
                sl = st % 2
                g = st // G_ST
                stl = st % G_ST
                for b4 in range(4):
                    slot = b4 % 2

                    def otr(t_, g=g, stl=stl, b4=b4, slot=slot):
                        dep.rd(t_, "t", b_qTg[g], b_consts)
                        dep.wr(t_, "t", b_psB[slot])
                        ob = qT[:, g * REG + stl * D:g * REG + (stl + 1) * D]
                        for b in range(4):
                            eb_ = b4 * 4 + b
                            li = t_.matmul(psB[:, slot * 512 + b * 128:
                                               slot * 512 + (b + 1) * 128],
                                           ob[:, eb_ * 128:(eb_ + 1) * 128],
                                           idb[:, :], start=True, stop=True)
                        tok = dep.inc(li, "t")
                        dep.did_w(tok, b_psB[slot])
                        dep.did_r(tok, b_qTg[g])

                    T(otr)

                    def oev(s, st=st, b4=b4, slot=slot, sl=sl):
                        dep.rd(s, "s", b_psB[slot])
                        dep.wr(s, "s", b_oT[sl])
                        tok = dep.inc(s.copy(
                            oT[sl][:, b4 * 4:(b4 + 1) * 4, :],
                            psB[:, slot * 512:(slot + 1) * 512]
                            .rearrange("p (a n) -> p a n", n=128)), "s")
                        b_oT[sl].w = tok
                        dep.did_r(tok, b_psB[slot])

                    S(oev)

            def emit_yproj(st):
                sl = st % 2

                def yproj(t_, sl=sl):
                    dep.rd(t_, "t", b_oT[sl])
                    dep.wr(t_, "t", b_psA)
                    for n in range(NCH):
                        if n == 0:
                            dep.rd(t_, "t", b_wo_c0)
                        else:
                            dep.rd(t_, "t", b_wo[n])
                        for et in range(ET):
                            src = (wo_c0[:, et, :] if n == 0 else
                                   wo_s[:, et, (n - 1) * 512:n * 512])
                            li = t_.matmul(psA[:, n * 512:(n + 1) * 512],
                                           oT[sl][:, et, :], src,
                                           start=(et == 0),
                                           stop=(et == ET - 1))
                    tok = dep.inc(li, "t")
                    dep.did_w(tok, b_psA)
                    dep.did_r(tok, b_oT[sl], b_wo_c0)
                    for n in range(1, NCH):
                        dep.did_r(tok, b_wo[n])

                T(yproj)

            emit_otr(0)
            emit_yproj(0)

            for st in range(NST):
                sl = st % 2
                g = st // G_ST
                stl = st % G_ST
                ln_o = 16 * sl
                b_lnc = b_lnp[sl]
                if st + 1 < NST:
                    Q(lambda q, st=st: xload(q, st + 1, (st + 1) % 2))

                def resid(v, sl=sl, ln_o=ln_o, b_lnc=b_lnc):
                    dep.rd(v, "v", b_psA, b_xt[sl])
                    dep.wr(v, "v", b_x1[sl], b_lnc)
                    tok = dep.inc(v.scalar_tensor_tensor(
                        x1[sl][:, :], psA[:, :], 1.0, xt[sl][:, :],
                        op0=ALU.mult, op1=ALU.add,
                        accum_out=ln[:, ln_o + 0:ln_o + 1]), "v")
                    dep.did_w(tok, b_x1[sl])
                    b_lnc.w = tok
                    dep.did_r(tok, b_psA, b_xt[sl])

                V(resid)

                if st + 1 < NST:
                    emit_otr(st + 1)
                    emit_yproj(st + 1)

                def l1(v, sl=sl, ln_o=ln_o, b_lnc=b_lnc):
                    dep.rd(v, "v", b_x1[sl], b_lnc)
                    dep.wr(v, "v", b_scr)
                    t = dep.inc(v.scalar_tensor_tensor(
                        scr[:, :], x1[sl][:, :], 1.0, x1[sl][:, :],
                        op0=ALU.mult, op1=ALU.mult,
                        accum_out=ln[:, ln_o + 1:ln_o + 2]), "v")
                    dep.did_w(t, b_scr)
                    dep.did_r(t, b_x1[sl])
                    v.wait_ge(dep.sem["v"], dep.cnt["v"])
                    v.tensor_scalar(ln[:, ln_o + 2:ln_o + 3],
                                    ln[:, ln_o + 0:ln_o + 1], 1.0 / D, None,
                                    op0=ALU.mult)
                    dep.inc(v.tensor_scalar(ln[:, ln_o + 3:ln_o + 4],
                                            ln[:, ln_o + 1:ln_o + 2],
                                            -1.0 / D, -EPS,
                                            op0=ALU.mult, op1=ALU.add), "v")
                    v.wait_ge(dep.sem["v"], dep.cnt["v"])
                    t = dep.inc(v.scalar_tensor_tensor(
                        ln[:, ln_o + 4:ln_o + 5], ln[:, ln_o + 2:ln_o + 3],
                        ln[:, ln_o + 2:ln_o + 3], ln[:, ln_o + 3:ln_o + 4],
                        op0=ALU.mult, op1=ALU.add), "v")
                    dep.did_w(t, b_lnc)

                V(l1)

                def l2(s, ln_o=ln_o, b_lnc=b_lnc):
                    dep.rd(s, "s", b_lnc)
                    t = dep.inc(s.activation(ln[:, ln_o + 5:ln_o + 6],
                                             ln[:, ln_o + 4:ln_o + 5],
                                             AF.Ln, scale=-1.0), "s")
                    s.wait_ge(dep.sem["s"], t[1])
                    t = dep.inc(s.activation(ln[:, ln_o + 6:ln_o + 7],
                                             ln[:, ln_o + 5:ln_o + 6],
                                             AF.Exp, scale=-0.5), "s")
                    dep.did_w(t, b_lnc)

                S(l2)

                def l3(v, ln_o=ln_o, b_lnc=b_lnc):
                    dep.rd(v, "v", b_lnc)
                    t = dep.inc(v.scalar_tensor_tensor(
                        ln[:, ln_o + 7:ln_o + 8], ln[:, ln_o + 2:ln_o + 3],
                        -1.0, ln[:, ln_o + 6:ln_o + 7],
                        op0=ALU.mult, op1=ALU.mult), "v")
                    dep.did_w(t, b_lnc)

                V(l3)

                def l4(s, sl=sl, ln_o=ln_o, b_lnc=b_lnc):
                    dep.rd(s, "s", b_lnc, b_x1[sl])
                    dep.wr(s, "s", b_xn2[sl])
                    t = dep.inc(s.activation(xn2[sl][:, :], x1[sl][:, :],
                                             AF.Identity,
                                             bias=ln[:, ln_o + 7:ln_o + 8],
                                             scale=ln[:, ln_o + 6:ln_o + 7]),
                                "s")
                    dep.did_w(t, b_xn2[sl])
                    dep.did_r(t, b_x1[sl], b_lnc)

                S(l4)

                for b4 in range(4):
                    slot = b4 % 2

                    def xtr(t_, sl=sl, b4=b4, slot=slot):
                        dep.rd(t_, "t", b_xn2[sl], b_consts)
                        dep.wr(t_, "t", b_psB[slot])
                        for b in range(4):
                            jt = b4 * 4 + b
                            li = t_.matmul(psB[:, slot * 512 + b * 128:
                                               slot * 512 + (b + 1) * 128],
                                           xn2[sl][:, jt * 128:(jt + 1) * 128],
                                           idb[:, :], start=True, stop=True)
                        tok = dep.inc(li, "t")
                        dep.did_w(tok, b_psB[slot])
                        dep.did_r(tok, b_xn2[sl])

                    T(xtr)

                    def xev(s, b4=b4, slot=slot):
                        dep.rd(s, "s", b_psB[slot])
                        dep.wr(s, "s", b_x2T)
                        tok = dep.inc(s.copy(
                            x2T[:, b4 * 4:(b4 + 1) * 4, :],
                            psB[:, slot * 512:(slot + 1) * 512]
                            .rearrange("p (a n) -> p a n", n=128)), "s")
                        b_x2T.w = tok
                        dep.did_r(tok, b_psB[slot])

                    S(xev)

                def logits(t_):
                    dep.rd(t_, "t", b_x2T, b_consts, b_rwc)
                    dep.wr(t_, "t", b_psSp)
                    for jt in range(ET):
                        li = t_.matmul(psB[:, 1536:1536 + NEXP],
                                       x2T[:, jt, :], rw_s[:, jt, :],
                                       start=(jt == 0), stop=False)
                    li = t_.matmul(psB[:, 1536:1536 + NEXP], onesb[0:1, :],
                                   rb_s[0:1, :], start=False, stop=True)
                    tok = dep.inc(li, "t")
                    dep.did_w(tok, b_psSp)
                    dep.did_r(tok, b_x2T)

                T(logits)

                def smax_v(v, ln_o=ln_o, b_lnc=b_lnc):
                    dep.rd(v, "v", b_psSp)
                    dep.wr(v, "v", b_lnc)
                    t0 = dep.inc(v.reduce_max(ln[:, ln_o + 8:ln_o + 9],
                                              psB[:, 1536:1536 + NEXP],
                                              axis=AX.X), "v")
                    v.wait_ge(dep.sem["v"], t0[1])
                    t = dep.inc(v.tensor_scalar(ln[:, ln_o + 9:ln_o + 10],
                                                ln[:, ln_o + 8:ln_o + 9],
                                                -1.0, None, op0=ALU.mult),
                                "v")
                    dep.did_w(t, b_lnc)
                    dep.did_r(t, b_psSp)

                V(smax_v)

                def smax_s(s, ln_o=ln_o, b_lnc=b_lnc):
                    dep.rd(s, "s", b_lnc, b_psSp)
                    dep.wr(s, "s", b_p)
                    t = dep.inc(s.activation(pp[:, :],
                                             psB[:, 1536:1536 + NEXP],
                                             AF.Exp,
                                             bias=ln[:, ln_o + 9:ln_o + 10],
                                             accum_out=ln[:, ln_o + 10:
                                                          ln_o + 11]), "s")
                    dep.did_w(t, b_p)
                    b_lnc.w = t
                    dep.did_r(t, b_psSp)

                S(smax_s)

                def topk(v, ln_o=ln_o, b_lnc=b_lnc):
                    dep.rd(v, "v", b_p, b_lnc)
                    dep.wr(v, "v", b_mp)
                    v.reciprocal(ln[:, ln_o + 11:ln_o + 12],
                                 ln[:, ln_o + 10:ln_o + 11])
                    v.max(top8[:, :], pp[:, :])
                    t = dep.inc(v.engine_nop(), "v")
                    v.wait_ge(dep.sem["v"], t[1])
                    v.match_replace(wk64[:, :], top8[:, :], pp[:, :], -1.0)
                    t = dep.inc(v.engine_nop(), "v")
                    v.wait_ge(dep.sem["v"], t[1])
                    v.tensor_tensor(msk[:, :], wk64[:, :], pp[:, :],
                                    op=ALU.not_equal)
                    t = dep.inc(v.engine_nop(), "v")
                    v.wait_ge(dep.sem["v"], t[1])
                    v.tensor_mul(mpb[:, :], msk[:, :], pp[:, :])
                    t = dep.inc(v.reduce_sum(ln[:, ln_o + 12:ln_o + 13],
                                             top8[:, :], axis=AX.X), "v")
                    b_lnc.w = t
                    dep.did_w(t, b_mp)
                    dep.did_r(t, b_p)

                V(topk)

                def mptr(t_):
                    dep.rd(t_, "t", b_mp, b_consts)
                    dep.wr(t_, "t", b_psRS)
                    li = t_.matmul(psB[0:64, 1024:1024 + 128], mpb[:, :],
                                   idb[:, :], start=True, stop=True)
                    tok = dep.inc(li, "t")
                    dep.did_w(tok, b_psRS)
                    dep.did_r(tok, b_mp)

                T(mptr)

                def mpev(s):
                    dep.rd(s, "s", b_psRS)
                    dep.wr(s, "s", b_mpT)
                    tok = dep.inc(s.copy(mpT[0:64, :],
                                         psB[0:64, 1024:1024 + 128]), "s")
                    dep.did_w(tok, b_mpT)
                    dep.did_r(tok, b_psRS)

                S(mpev)

                for n in range(NCH):
                    bb = b_psRS if n % 2 == 0 else b_psSp
                    boff = 1024 if n % 2 == 0 else 1536

                    def bmat(t_, n=n, bb=bb, boff=boff):
                        dep.rd(t_, "t", b_mpT, b_rwc)
                        dep.wr(t_, "t", bb)
                        li = t_.matmul(psB[:, boff:boff + 512], mpT[0:64, :],
                                       eb_s[0:64, n * 512:(n + 1) * 512],
                                       start=True, stop=True)
                        tok = dep.inc(li, "t")
                        dep.did_w(tok, bb)
                        dep.did_r(tok, b_mpT)

                    T(bmat)

                    def ffn_c(v, sl=sl, n=n, bb=bb, boff=boff, ln_o=ln_o,
                              b_lnc=b_lnc):
                        dep.rd(v, "v", bb, b_xn2[sl], b_lnc)
                        dep.wr(v, "v", b_xt[sl])
                        tok = dep.inc(v.scalar_tensor_tensor(
                            xt[sl][:, n * 512:(n + 1) * 512],
                            xn2[sl][:, n * 512:(n + 1) * 512],
                            ln[:, ln_o + 12:ln_o + 13],
                            psB[:, boff:boff + 512],
                            op0=ALU.mult, op1=ALU.add), "v")
                        b_xt[sl].w = tok
                        dep.did_r(tok, bb, b_xn2[sl])

                    V(ffn_c)

                def ffn_out(v, sl=sl, ln_o=ln_o, b_lnc=b_lnc):
                    dep.rd(v, "v", b_xt[sl], b_x1[sl], b_lnc)
                    dep.wr(v, "v", b_x2)
                    tok = dep.inc(v.scalar_tensor_tensor(
                        x2[:, :], xt[sl][:, :], ln[:, ln_o + 11:ln_o + 12],
                        x1[sl][:, :], op0=ALU.mult, op1=ALU.add), "v")
                    dep.did_w(tok, b_x2)
                    dep.did_r(tok, b_xt[sl], b_x1[sl], b_lnc)

                V(ffn_out)

                def ywrite(q, st=st):
                    dep.rd(q, "q", b_x2)
                    tok = dep.inc(q.dma_start(
                        out=y_h[st * 128:(st + 1) * 128, :],
                        in_=x2[:, :]), "yw0", 16)
                    dep.did_r(tok, b_x2)

                Q(ywrite)

            def fin(q):
                q.wait_ge(dep.sem["yw0"], dep.cnt["yw0"])

            Q(fin)

    return nc


def _host_prep(x, qw, qb, kvw, kvb, ow, rw, rb, eb, S_CORE):
    S_TOT = NCORE * S_CORE
    f32 = np.float32
    bf16 = ml_dtypes.bfloat16

    xs = np.ascontiguousarray(np.asarray(x, dtype=f32).reshape(S_TOT, D))
    wq_pack = np.ascontiguousarray(
        np.asarray(qw, f32).T.reshape(ET, 128, ET, 128).transpose(2, 1, 0, 3))
    kw = np.asarray(kvw, f32)[:D, :]
    vw = np.asarray(kvw, f32)[D:, :]
    wk_pack = np.ascontiguousarray(
        kw.T.reshape(ET, 128, ET, 128).transpose(2, 1, 0, 3))
    wv_mov = np.ascontiguousarray(vw.T).reshape(ET, 128, D)
    wo_mov = np.ascontiguousarray(np.asarray(ow, f32).T).reshape(ET, 128, D)
    qb_t = np.ascontiguousarray(np.asarray(qb, f32).reshape(ET, 128).T)
    kb_t = np.ascontiguousarray(np.asarray(kvb, f32)[:D].reshape(ET, 128).T)
    vb_row = np.ascontiguousarray(np.asarray(kvb, f32)[D:].reshape(1, D))
    rw_pack = np.ascontiguousarray(np.asarray(rw, f32).T).reshape(ET, 128,
                                                                  NEXP)
    rb_row = np.ascontiguousarray(np.asarray(rb, f32).reshape(1, NEXP))
    eb_a = np.ascontiguousarray(np.asarray(eb, f32))

    half = 64
    inv = 1.0 / (10000.0 ** (np.arange(half, dtype=np.float64) / half))
    pos = np.arange(S_TOT, dtype=np.float64)
    fr = pos[:, None] * inv[None, :]
    emb = np.concatenate([fr, fr], axis=1)
    cosT = np.ascontiguousarray(np.cos(emb).T.astype(f32))
    sinT = np.ascontiguousarray(np.sin(emb).T.astype(f32))
    sc = f32(1.0 / np.sqrt(128.0))
    cosq = (cosT * sc).astype(f32)
    sinq = (sinT * sc).astype(f32)

    idb = np.eye(128, dtype=bf16)
    rotm = np.zeros((128, 128), dtype=f32)
    for m in range(64):
        rotm[m + 64, m] = -1.0
        rotm[m, m + 64] = 1.0
    rotm = rotm.astype(bf16)
    onesb = np.ones((128, 128), dtype=bf16)
    onesf = np.ones((1, 1), dtype=f32)

    maps = []
    for c in range(NCORE):
        s0, s1 = c * S_CORE, (c + 1) * S_CORE
        maps.append(dict(
            x=np.ascontiguousarray(xs[s0:s1]),
            wq=wq_pack, wk=wk_pack, wv=wv_mov, wo=wo_mov,
            qb=qb_t, kb=kb_t, vb=vb_row,
            rw=rw_pack, rb=rb_row, eb=eb_a,
            cosq=np.ascontiguousarray(cosq[:, s0:s1]),
            sinq=np.ascontiguousarray(sinq[:, s0:s1]),
            cosk=np.ascontiguousarray(cosT[:, s0:s1]),
            sink=np.ascontiguousarray(sinT[:, s0:s1]),
            idb=idb, rotm=rotm, onesb=onesb, onesf=onesf,
        ))
    return maps


def kernel(L0_attn_x, L0_attn_q_w, L0_attn_q_b, L0_attn_kv_w, L0_attn_kv_b,
           L0_attn_o_w, L0_ffn_router_weight, L0_ffn_router_bias,
           L0_ffn_expert_bias):
    S_CORE = 2048
    maps = _host_prep(L0_attn_x, L0_attn_q_w, L0_attn_q_b, L0_attn_kv_w,
                      L0_attn_kv_b, L0_attn_o_w, L0_ffn_router_weight,
                      L0_ffn_router_bias, L0_ffn_expert_bias, S_CORE)
    nc = build(S_CORE=S_CORE, G_ST=4, T_SUP=4)
    res = run_bass_kernel_spmd(nc, maps, core_ids=list(range(NCORE)))
    y = np.concatenate([r["y"] for r in res.results], axis=0)
    return y.reshape(1, NCORE * S_CORE, D).astype(np.float32)
